# version 17
# baseline (speedup 1.0000x reference)
import numpy as np
from contextlib import ExitStack

import concourse.bass as bass
import concourse.mybir as mybir
from concourse.bass_utils import run_bass_kernel_spmd

F32 = mybir.dt.float32
BF16 = mybir.dt.bfloat16
AF = mybir.ActivationFunctionType
ALU = mybir.AluOpType
AXX = mybir.AxisListType.X

NCORES = 8
L = 128


class CFG:
    def __init__(self, d_model, nh, t_core, mini=False):
        self.mini = mini
        self.D = d_model
        self.KC = d_model // 128
        self.NH = nh
        self.DSSM = nh * 128
        self.CONV = self.DSSM + 256
        self.NCV = nh + 2
        self.PROJ = self.DSSM + self.CONV + nh
        self.T = t_core
        self.TH = t_core // 2
        self.NSUB = self.TH // 512
        self.NCH = self.TH // L
        self.S = t_core * NCORES
        self.RANK = 32
        self.NE = 8
        assert self.TH % 512 == 0


FULL = CFG(4096, 64, 2048)
MINI = CFG(512, 8, 1024, mini=True)
MID = CFG(512, 8, 2048, mini=True)


def _tile_lhs(w, kc, mch, mw=128):
    K, M = w.shape
    assert K == kc * 128 and M == mch * mw
    t = w.reshape(kc, 128, mch, mw)
    t = np.transpose(t, (2, 1, 0, 3))
    return np.ascontiguousarray(t.reshape(mch, 128, kc * mw), dtype=np.float32)


def shard_inputs(cfg, inp):
    x = np.asarray(inp["L0_attn_hidden_states"], dtype=np.float32).reshape(cfg.S, cfg.D)
    W_in = np.asarray(inp["L0_attn_W_in"], dtype=np.float32)
    conv_w = np.asarray(inp["L0_attn_conv_w"], dtype=np.float32)
    NH, KC, DSSM = cfg.NH, cfg.KC, cfg.DSSM
    Wf = np.asarray(inp["L0_ffn_W"], dtype=np.float32)
    base = {
        "win_z": _tile_lhs(W_in[:, :DSSM], KC, NH),
        "win_xbc": _tile_lhs(W_in[:, DSSM:DSSM + cfg.CONV], KC, cfg.NCV),
        "wdt": _tile_lhs(W_in[:, DSSM + cfg.CONV:], KC, 1, mw=NH)[0],
        "wout": _tile_lhs(np.asarray(inp["L0_attn_W_out"], dtype=np.float32), NH, KC),
        "wff": _tile_lhs(np.ascontiguousarray(Wf.T), KC, KC),
        "w8t": _tile_lhs(np.ascontiguousarray(Wf[:cfg.NE, :].T), KC, 1, mw=cfg.NE)[0],
        "wa": _tile_lhs(np.asarray(inp["L0_ffn_A"], dtype=np.float32), KC, 1,
                        mw=cfg.RANK)[0],
        "wbm": np.ascontiguousarray(inp["L0_ffn_Bmat"], dtype=np.float32),
        "lim": np.ascontiguousarray(inp["L0_ffn_LiMEs"], dtype=np.float32),
        "limt": np.ascontiguousarray(
            np.asarray(inp["L0_ffn_LiMEs"], dtype=np.float32).T),
        "wcv": np.ascontiguousarray(
            conv_w.reshape(cfg.NCV, 128, 4).transpose(1, 0, 2).reshape(128, cfg.NCV * 4),
            dtype=np.float32),
        "nw_t": np.ascontiguousarray(
            np.asarray(inp["L0_attn_norm_weight"], dtype=np.float32).reshape(NH, 128).T),
        "bcol": np.ascontiguousarray(
            np.asarray(inp["L0_ffn_b"], dtype=np.float32).reshape(KC, 128).T),
        "dtb": np.asarray(inp["L0_attn_dt_bias"], dtype=np.float32).reshape(NH, 1),
        "alog": np.asarray(inp["L0_attn_A_log"], dtype=np.float32).reshape(NH, 1),
        "dpr": np.asarray(inp["L0_attn_D_param"], dtype=np.float32).reshape(1, NH),
        "eye": np.eye(128, dtype=np.float32),
        "tri": np.ascontiguousarray(np.triu(np.ones((128, 128), dtype=np.float32))),
    }
    per_core = []
    for i in range(NCORES):
        lo = i * cfg.T
        xs = np.zeros((cfg.T + 3, cfg.D), dtype=np.float32)
        src_lo = max(0, lo - 3)
        xs[3 - (lo - src_lo):] = x[src_lo:lo + cfg.T]
        cm = np.zeros((NH, 8), dtype=np.float32)
        cm[:, :i] = 1.0
        m = dict(base)
        m["xt"] = np.ascontiguousarray(xs.T)
        m["cmask"] = cm
        per_core.append(m)
    return per_core


ENGS = ("tensor", "vector", "scalar", "sync", "gpsimd")


class B:
    def __init__(self, nc):
        self.nc = nc
        self.ops = {e: [] for e in ENGS}
        self.sems = {}
        self.cnt = {}
        self.wm = {e: {} for e in ENGS}
        self.es = ExitStack()

    def sem(self, name):
        self.sems[name] = self.es.enter_context(self.nc.semaphore(name))
        self.cnt[name] = 0
        return name

    def op(self, eng, fn, deps=(), sem=None, k=1):
        for d in deps:
            if d is None:
                continue
            s, v = d
            if v <= 0 or self.wm[eng].get(s, 0) >= v:
                continue
            self.wm[eng][s] = v
            h = self.sems[s]
            self.ops[eng].append(lambda e, h=h, v=v: e.wait_ge(h, v))
        if fn is None:
            return None
        if sem is None:
            self.ops[eng].append(fn)
            return None
        self.cnt[sem] += k
        v = self.cnt[sem]
        h = self.sems[sem]
        self.ops[eng].append(lambda e, fn=fn, h=h, k=k: fn(e).then_inc(h, k))
        return (sem, v)

    def tok(self, sem):
        return (sem, self.cnt[sem])

    def finalize(self, block):
        for eng in ENGS:
            thunks = self.ops[eng]
            if not thunks:
                continue

            def run(e, thunks=thunks):
                for fn in thunks:
                    fn(e)
            getattr(block, eng)(run)


class Pool:

    def __init__(self, b, nc, es):
        self.b = b
        self.f32 = [es.enter_context(nc.psum_tensor(f"psb{i}", [128, 512], F32))
                    for i in range(6)]
        self.bf = [es.enter_context(nc.psum_tensor(f"psbf{i}", [128, 1024], BF16))
                   for i in range(2)]
        self.rd = [b.sem(f"psr{i}") for i in range(8)]
        self.wr = [b.sem(f"psw{i}") for i in range(8)]
        self.rr = 0
        self.pinned = set()

    def acquire(self, pin=False):
        for _ in range(8):
            i = self.rr
            self.rr = (self.rr + 1) % 6
            if i not in self.pinned:
                if pin:
                    self.pinned.add(i)
                return i
        raise RuntimeError("no free bank")

    def guard_tok(self, i):
        return (self.rd[i], self.b.cnt[self.rd[i]])

    def unpin(self, i):
        self.pinned.discard(i)


def build_program(cfg: CFG):
    nc = bass.Bass("TRN2", target_bir_lowering=False)
    KC, NH, TH, NCH, NSUB = cfg.KC, cfg.NH, cfg.TH, cfg.NCH, cfg.NSUB
    T, NCV, NE, RANK = cfg.T, cfg.NCV, cfg.NE, cfg.RANK
    W = TH + 3

    def din(name, shape, dtype=F32):
        return nc.dram_tensor(name, shape, dtype, kind="ExternalInput")

    def dint(name, shape, dtype=F32):
        return nc.dram_tensor(name, shape, dtype)

    xt = din("xt", [cfg.D, T + 3])
    win_z = din("win_z", [NH, 128, KC * 128])
    win_xbc = din("win_xbc", [NCV, 128, KC * 128])
    wdtd = din("wdt", [128, KC * NH])
    woutd = din("wout", [KC, 128, NH * 128])
    wffd = din("wff", [KC, 128, KC * 128])
    w8td = din("w8t", [128, KC * NE])
    wad = din("wa", [128, KC * RANK])
    wbmd = din("wbm", [RANK, cfg.D])
    limd = din("lim", [NE, RANK])
    limtd = din("limt", [RANK, NE])
    wcvd = din("wcv", [128, NCV * 4])
    nwd = din("nw_t", [128, NH])
    bcold = din("bcol", [128, KC])
    dtbd = din("dtb", [NH, 1])
    alogd = din("alog", [NH, 1])
    dprd = din("dpr", [1, NH])
    cmaskd = din("cmask", [NH, 8])
    eyed = din("eye", [128, 128])
    trid = din("tri", [128, 128])
    ytd = nc.dram_tensor("yt", [cfg.D, T], F32, kind="ExternalOutput")

    sc_lam = dint("sc_lam", [2, NH, TH])
    sc_xc = dint("sc_xc", [NH, 2, 128, TH], BF16)
    sc_zs = dint("sc_zs", [NH, 2, 128, TH], BF16)
    sc_yp = dint("sc_yp", [NH, 2, 128, TH], BF16)
    sc_cd2 = dint("sc_cd2", [NH, 2, 128, TH], BF16)
    sc_x1 = dint("sc_x1", [KC, 2, 128, TH])
    sc_x1b = dint("sc_x1b", [KC, 2, 128, TH], BF16)
    sc_r = dint("sc_r", [KC, 2, 128, TH])
    sc_ds = dint("sc_ds", [NH, 8])
    sc_hin = dint("sc_hin", [NH, 128, 128], BF16)
    cc1_in = dint("cc1_in", [NH * 128 + 1, 128])
    cc1_out = dint("cc1_out", [NCORES * (NH * 128 + 1), 128])
    cc2_in = dint("cc2_in", [8, 4])
    cc2_out = dint("cc2_out", [NCORES * 8, 4])

    b = B(nc)
    es = b.es
    EV = b.sem("ev")
    AC = b.sem("ac")
    CC = b.sem("cc")
    NL = 8
    qlanes = {"sync": [b.sem(f"sl{i}") for i in range(NL)],
              "gpsimd": [b.sem(f"gl{i}") for i in range(NL)]}
    qcount = {"sync": [0], "gpsimd": [0]}
    out_dma_toks = []
    pool = Pool(b, nc, es)

    SB_BYTES = 207 * 1024
    arena = es.enter_context(nc.sbuf_tensor("arena", [128, SB_BYTES], mybir.dt.uint8))
    arena_base = nc.lookup_mloc(arena).addr
    cur = [0]

    def salloc(name, shape, dtype, at=None):
        per_part = int(np.prod(shape[1:])) * mybir.dt.size(dtype)
        if at is None:
            addr = (cur[0] + 31) // 32 * 32
            cur[0] = addr + per_part
            assert cur[0] <= SB_BYTES, f"SBUF overflow at {name}: {cur[0]}"
        else:
            addr = at
            assert addr + per_part <= SB_BYTES
        h = nc.virtual_sbuf_tensor(name, shape, dtype)
        nc._tensor(h.name, h.shape, h.dtype, type="SB")
        mloc = nc.lookup_mloc(h)
        mloc.allocated = True
        mloc.addr = arena_base + addr
        return h

    def s_offset(h):
        return nc.lookup_mloc(h).addr - arena_base

    eye_f = salloc("eye_f", [128, 128], F32)
    eye_b = salloc("eye_b", [128, 128], BF16)
    tri_b = salloc("tri_b", [128, 128], BF16)
    ones_f = salloc("ones_f", [1, 128], F32)
    ones_b = salloc("ones_b", [1, 128], BF16)
    onescol_b = salloc("onescol_b", [128, 1], BF16)
    wcv_s = salloc("wcv_s", [128, NCV * 4], F32)
    nw_s = salloc("nw_s", [128, NH], F32)
    bcol_s = salloc("bcol_s", [128, KC], F32)
    dtb_s = salloc("dtb_s", [NH, 1], F32)
    nea_s = salloc("nea_s", [NH, 1], F32)
    dpr_s = salloc("dpr_s", [1, NH], F32)
    cmask_s = salloc("cmask_s", [NH, 8], F32)
    dprefst = salloc("dprefst", [128, NH], F32)
    mean1r = salloc("mean1r", [1, T], F32)
    rstd1r = salloc("rstd1r", [1, T], F32)
    rmsr = salloc("rmsr", [1, T], F32)
    dcol128 = salloc("dcol128", [128, 2], F32)
    cst_s = salloc("cst_s", [128, 2], F32)
    dtot_row = salloc("dtot_row", [1, 128], F32)
    z8_s = salloc("z8_s", [8, 4], F32)
    PH = cur[0]

    def dma(q, dst, src, deps=(), slow=False):
        lanes = qlanes[q]
        lane = qcount[q][0] % NL
        qcount[q][0] += 1
        sem = lanes[lane]
        prev = (sem, b.cnt[sem])
        return b.op(q, lambda e, d=dst, s=src, sl=slow: e.dma_start(
            out=d, in_=s, allow_slow_non_contiguous=sl),
            deps=tuple(deps) + (prev,), sem=sem, k=16)

    def dma_out(dst, src, deps=()):
        t = dma("sync", dst, src, deps=deps)
        out_dma_toks.append(t)
        return t

    def vop(fn, deps=()):
        return b.op("vector", fn, deps=deps, sem=EV)

    def sop(fn, deps=()):
        return b.op("scalar", fn, deps=deps, sem=AC)

    def mm_group(items, deps=(), M=128):
        bk = pool.acquire()
        deps = tuple(deps) + (pool.guard_tok(bk),)
        n = len(items)
        tok = None
        for j, (lf, rf, lo, nn) in enumerate(items):
            st, sp = (j == 0), (j == n - 1)
            fn = (lambda e, lf=lf, rf=rf, lo=lo, nn=nn, st=st, sp=sp, bk=bk:
                  e.matmul(pool.f32[bk][0:M, lo:lo + nn], lf(), rf(), start=st, stop=sp))
            if sp:
                tok = b.op("tensor", fn, deps=deps if st else (), sem=pool.wr[bk])
            else:
                b.op("tensor", fn, deps=deps if st else ())
        return bk, tok

    def mm1(out_fn, lhs_fn, rhs_fn, deps=()):
        bk = pool.acquire()
        deps = tuple(deps) + (pool.guard_tok(bk),)
        tok = b.op("tensor",
                   lambda e, bk=bk: e.matmul(out_fn(bk), lhs_fn(), rhs_fn(),
                                             start=True, stop=True),
                   deps=deps, sem=pool.wr[bk])
        return bk, tok

    def read_bank(eng, bk, fn, deps=()):
        prev = (pool.rd[bk], b.cnt[pool.rd[bk]])
        return b.op(eng, fn, deps=tuple(deps) + (prev,), sem=pool.rd[bk])

    def transpose_f32(in_fn, n_cols, deps=(), in_parts=128, out_parts=128):
        bk = pool.acquire()
        deps = tuple(deps) + (pool.guard_tok(bk),)
        tok = b.op("tensor",
                   lambda e, bk=bk: e.transpose(pool.f32[bk][0:out_parts, 0:n_cols],
                                                in_fn(),
                                                eye_f[0:in_parts, 0:in_parts]),
                   deps=deps, sem=pool.wr[bk])
        return bk, tok

    def transpose_bf(in_fn, n_cols, slot, deps=()):
        bk = 6 + slot
        deps = tuple(deps) + (pool.guard_tok(bk),)
        tok = b.op("tensor",
                   lambda e, n=n_cols, slot=slot: e.transpose(
                       pool.bf[slot][:, 0:n], in_fn(), eye_b[:, :]),
                   deps=deps, sem=pool.wr[bk])
        return bk, tok

    def barrier_all():
        toks = [(s, b.cnt[s]) for s in b.sems]
        for eng in ENGS:
            b.op(eng, None, deps=toks)

    class AccGroup:

        def __init__(self, M=1, N=512):
            self.bk = pool.acquire(pin=True)
            self.M, self.N = M, N
            self.first = True
            self.tok = None

        def add(self, rhs_fn, lhs_fn, deps=(), last=False):
            gd = (pool.guard_tok(self.bk),) if self.first else ()
            st = self.first
            self.first = False
            self.tok = b.op("tensor",
                            lambda e, bk=self.bk, st=st, lf=lhs_fn, rf=rhs_fn,
                            sp=last: e.matmul(pool.f32[bk][0:self.M, 0:self.N],
                                              lf(), rf(), start=st, stop=sp),
                            deps=tuple(deps) + gd, sem=pool.wr[self.bk])
            return self.tok

        def close(self):
            pool.unpin(self.bk)

    t_eye = dma("sync", eye_f[:, :], eyed[:, :])
    t_wcv = dma("sync", wcv_s[:, :], wcvd[:, :])
    t_nw = dma("sync", nw_s[:, :], nwd[:, :])
    t_bc = dma("sync", bcol_s[:, :], bcold[:, :])
    t_dtb = dma("sync", dtb_s[:, :], dtbd[:, :])
    t_alog = dma("sync", nea_s[:, :], alogd[:, :])
    t_dpr = dma("sync", dpr_s[:, :], dprd[:, :])
    t_cm = dma("sync", cmask_s[:, :], cmaskd[:, :])
    t_eyeb = dma("gpsimd", eye_b[:, :], eyed[:, :])
    t_trib = dma("gpsimd", tri_b[:, :], trid[:, :])

    b.op("vector", lambda e: e.memset(cst_s[:, 0:1], 1.0))
    b.op("vector", lambda e: e.memset(cst_s[:, 1:2], 1e-5))
    b.op("vector", lambda e: e.memset(ones_f[:, :], 1.0))
    b.op("vector", lambda e: e.memset(ones_b[:, :], 1.0))
    b.op("vector", lambda e: e.memset(onescol_b[:, :], 1.0))
    b.op("vector", lambda e: e.memset(dprefst[:, :], 1.0))
    t_msets = vop(lambda e: e.memset(hstate[:, :], 0.0))
    t_ea = sop(lambda e: e.activation(nea_s[:, :], nea_s[:, :], AF.Exp),
               deps=(t_alog,))
    t_nea = vop(lambda e: e.tensor_scalar(nea_s[:, :], nea_s[:, :], -1.0, None,
                                          ALU.mult), deps=(t_ea,))
    t_consts = b.tok(EV)
    for _eng in ENGS:
        b.op(_eng, None, deps=(t_consts,))

    cur[0] = PH
    hstate = salloc("hstate", [128, NH * 128], F32)
    dall128 = salloc("dall128", [128, NH], F32)
    xb_s = salloc("xb_s", [128, KC * W], BF16)
    winb = [salloc(f"winb{i}", [128, KC * 128], BF16) for i in range(2)]
    convin = [salloc("convin0", [128, W], BF16)] * 2
    halo_s = salloc("halo_s", [128, NCV * 4], BF16)
    cstage = [salloc(f"cstage{i}", [128, TH], BF16) for i in range(2)]
    dt_u = salloc("dt_u", [NH, TH], F32)
    dt_t1 = salloc("dt_t1", [NH, TH], F32)
    dt_t2 = salloc("dt_t2", [NH, TH], F32)
    dts_s = dt_t2
    lam_sb = dt_u
    dtT_s = salloc("dtT_s", [128, NCH * NH], F32)
    diag_s = [salloc(f"diag{i}", [128, 128], BF16) for i in range(4)]
    bfm_s = salloc("bfm_s", [128, TH], BF16)
    cfm_s = salloc("cfm_s", [128, TH], BF16)
    bt_s = salloc("bt_s", [128, NCH * 128], BF16)
    gm_s = salloc("gm_s", [128, NCH * 128], BF16)
    lamL = [salloc(f"lamL{i}", [2, TH], F32) for i in range(2)]
    lamR = [salloc(f"lamR{i}", [2, TH], F32) for i in range(2)]
    erow = [salloc(f"erow{i}", [1, TH], BF16) for i in range(2)]
    dcols = [salloc(f"dcols{i}", [128, NCH], F32) for i in range(2)]
    xcvh = [salloc(f"xcvh{i}", [128, TH], BF16) for i in range(2)]
    ypstrip = [salloc("ypstrip0", [128, TH], BF16)] * 2
    cd2strip = [salloc("cd2strip0", [128, TH], BF16)] * 2
    mD_s = salloc("mD_s", [128, NCH * 128], F32, at=s_offset(dt_t1))
    wmB_s = [salloc("wmB_s0", [128, NCH * 128], BF16, at=s_offset(dt_t2)),
             salloc("wmB_s1", [128, NCH * 128], BF16,
                    at=s_offset(dt_t2) + NCH * 128 * 2)]
    dtxB_s = [salloc("dtxB_s0", [128, NCH * 128], BF16, at=s_offset(dt_u)),
              salloc("dtxB_s1", [128, NCH * 128], BF16,
                     at=s_offset(dt_u) + NCH * 128 * 2)]
    bfB_s = [salloc(f"bfB_s{i}", [128, NCH * 128], BF16) for i in range(2)]
    cdB_s = [salloc(f"cdB_s{i}", [128, NCH * 128], BF16) for i in range(2)]
    hb_s = [salloc(f"hb_s{i}", [128, 128], BF16) for i in range(2)]
    assert cur[0] <= SB_BYTES

    bkDP, tkDP = mm1(lambda bk: pool.f32[bk][:, 0:NH],
                     lambda: ones_f[:, :], lambda: dpr_s[0:1, 0:NH],
                     deps=(t_dpr,))
    t_dall = read_bank("vector", bkDP,
                       lambda e, bk=bkDP: e.tensor_copy(dall128[:, 0:NH],
                                                        pool.f32[bk][:, 0:NH]),
                       deps=(tkDP,))
    for i in range(2):
        b.op("vector", lambda e, i=i: e.memset(lamL[i][:, :], 1.0))
        b.op("vector", lambda e, i=i: e.memset(lamR[i][:, :], -1.0))
    t_lamset = vop(lambda e: e.nop())

    fr = {"winb": [None, None], "convin": [None, None], "cstage": [None, None],
          "lam": [None, None], "xcvh": [None, None], "strip": [None, None],
          "xb": None, "dtpass": None, "dtT": None, "lamsb": None,
          "conv_pe": None, "diag": None, "erow": [None, None], "ypsp": None,
          "cdsp": None, "dcp": [None, None]}
    hu_tok = [t_msets] * NH
    hb_tok = [None] * NH
    last_yp_sp = [None] * NH
    last_cd_sp = [None] * NH

    def xsl(kc, lo, hi):
        return xb_s[:, kc * W + lo: kc * W + hi]

    def phase1_half(half):
        t_x_l = []
        xtr = xt.ap().rearrange("(c p) w -> p c w", p=128)
        XB = 4
        for kc0 in range(0, KC, XB):
            nb = min(XB, KC - kc0)
            t_x_l.append(dma("gpsimd", xb_s[:, kc0 * W:(kc0 + nb) * W],
                             xtr[:, kc0:kc0 + nb, TH * half: TH * half + W],
                             deps=(fr["xb"],) if kc0 == 0 else ()))
        t_x = tuple(t_x_l)
        t_w = dma("gpsimd", winb[0][:, 0:KC * NH], wdtd[:, :],
                  deps=(fr["winb"][0],))
        ev_toks = []
        last_dt_tok = None
        for sub in range(NSUB):
            items = [((lambda kc=kc: winb[0][:, kc * NH:(kc + 1) * NH]),
                      (lambda kc=kc, sub=sub: xsl(kc, 3 + sub * 512, 3 + (sub + 1) * 512)),
                      0, 512) for kc in range(KC)]
            bk, tk = mm_group(items, deps=(t_w,) + t_x, M=NH)
            last_dt_tok = tk
            ev_toks.append(read_bank("scalar", bk,
                                     lambda e, bk=bk, sub=sub: e.activation(
                                         dt_u[0:NH, sub * 512:(sub + 1) * 512],
                                         pool.f32[bk][0:NH, 0:512], AF.Identity,
                                         bias=dtb_s[:, :]),
                                     deps=(tk, t_dtb, fr["dtpass"])))
        fr["winb"][0] = last_dt_tok
        t1 = sop(lambda e: e.activation(dt_t1[0:NH, :], dt_u[0:NH, :], AF.Abs),
                 deps=tuple(ev_toks) + (fr["dtpass"],))
        t2 = sop(lambda e: e.activation(dt_t2[0:NH, :], dt_t1[0:NH, :], AF.Exp,
                                        scale=-1.0), deps=(t1,))
        t3 = sop(lambda e: e.activation(dt_t2[0:NH, :], dt_t2[0:NH, :], AF.Ln,
                                        bias=cst_s[0:NH, 0:1]), deps=(t2,))
        t4 = vop(lambda e: e.tensor_scalar(dt_t1[0:NH, :], dt_u[0:NH, :], 0.0, None,
                                           ALU.max), deps=(t3,))
        t5 = vop(lambda e: e.tensor_tensor(dts_s[0:NH, :], dt_t1[0:NH, :],
                                           dt_t2[0:NH, :], ALU.add), deps=(t4,))
        t6 = vop(lambda e: e.tensor_scalar(dt_t1[0:NH, :], dts_s[0:NH, :],
                                           nea_s[:, :], None, ALU.mult),
                 deps=(t5, t_nea))
        t_lam = None
        for c in range(NCH):
            t_lam = vop(lambda e, c=c: e.tensor_tensor_scan(
                lam_sb[0:NH, c * L:(c + 1) * L], dt_t1[0:NH, c * L:(c + 1) * L],
                dt_t1[0:NH, c * L:(c + 1) * L], 0.0, ALU.add, ALU.bypass),
                deps=(t6, fr["lamsb"]))
        t_lamsp = dma("sync", sc_lam[half, 0:NH, :], lam_sb[0:NH, :], deps=(t_lam,))
        fr["lamsb"] = t_lamsp
        t_dtT = None
        for c in range(NCH):
            bk, tk = transpose_f32(lambda c=c: dts_s[0:NH, c * L:(c + 1) * L], NH,
                                   deps=(t5, t_eye), in_parts=NH)
            t_dtT = read_bank("vector", bk,
                              lambda e, bk=bk, c=c: e.tensor_copy(
                                  dtT_s[:, c * NH:(c + 1) * NH], pool.f32[bk][:, 0:NH]),
                              deps=(tk, fr["dtT"]))
        fr["dtpass"] = t_dtT

        def conv_chunk(ci, dst_kind, buf):
            t_w = dma("gpsimd", winb[buf][:, :], win_xbc[ci, :, :],
                      deps=(fr["winb"][buf],))
            cideps = (fr["convin"][buf],)
            t_ev = None
            proj_tok = None
            for sub in range(NSUB):
                items = [((lambda kc=kc, buf=buf: winb[buf][:, kc * 128:(kc + 1) * 128]),
                          (lambda kc=kc, sub=sub: xsl(kc, 3 + sub * 512, 3 + (sub + 1) * 512)),
                          0, 512) for kc in range(KC)]
                bk, tk = mm_group(items, deps=(t_w,) + t_x)
                proj_tok = tk
                t_ev = read_bank("scalar", bk,
                                 lambda e, bk=bk, buf=buf, sub=sub: e.activation(
                                     convin[buf][:, 3 + sub * 512: 3 + (sub + 1) * 512],
                                     pool.f32[bk][:, 0:512], AF.Copy),
                                 deps=(tk,) + cideps)
            if half == 0:
                items = [((lambda kc=kc, buf=buf: winb[buf][:, kc * 128:(kc + 1) * 128]),
                          (lambda kc=kc: xsl(kc, 0, 3)), 0, 3) for kc in range(KC)]
                bk2, tk2 = mm_group(items, deps=())
                proj_tok = tk2
                t_h = read_bank("scalar", bk2,
                                lambda e, bk=bk2, buf=buf: e.activation(
                                    convin[buf][:, 0:3], pool.f32[bk][:, 0:3],
                                    AF.Copy),
                                deps=(tk2,) + cideps)
            else:
                t_h = vop(lambda e, ci=ci, buf=buf: e.tensor_copy(
                    convin[buf][:, 0:3], halo_s[:, ci * 4: ci * 4 + 3]),
                    deps=cideps)
            fr["winb"][buf] = proj_tok
            if half == 0:
                vop(lambda e, ci=ci, buf=buf: e.tensor_copy(
                    halo_s[:, ci * 4: ci * 4 + 3], convin[buf][:, TH: TH + 3]),
                    deps=(t_ev,))
            t_dg = None
            for k in range(4):
                t_dg = vop(lambda e, k=k, ci=ci: e.tensor_scalar(
                    diag_s[k][:, :], eye_b[:, :],
                    wcv_s[:, ci * 4 + k: ci * 4 + k + 1], None, ALU.mult),
                    deps=(t_eyeb, t_wcv, fr["conv_pe"]))
            last_cv = None
            conv_tok = None
            for sub in range(NSUB):
                items = [((lambda k=k: diag_s[k][:, :]),
                          (lambda k=k, sub=sub, buf=buf:
                           convin[buf][:, sub * 512 + k: sub * 512 + k + 512]),
                          0, 512) for k in range(4)]
                bk, tk = mm_group(items, deps=(t_ev, t_h, t_dg))
                conv_tok = tk
                last_cv = read_bank("scalar", bk,
                                    lambda e, bk=bk, buf=buf, sub=sub: e.activation(
                                        cstage[buf][:, sub * 512:(sub + 1) * 512],
                                        pool.f32[bk][:, 0:512], AF.Silu),
                                    deps=(tk, fr["cstage"][buf]))
            fr["conv_pe"] = conv_tok
            fr["convin"][0] = fr["convin"][1] = conv_tok
            if dst_kind == "B":
                t_cv = vop(lambda e: e.tensor_copy(bfm_s[:, :], cstage[buf][:, :]),
                           deps=(last_cv,))
                fr["cstage"][buf] = t_cv
                return t_cv
            if dst_kind == "C":
                t_cv = vop(lambda e: e.tensor_copy(cfm_s[:, :], cstage[buf][:, :]),
                           deps=(last_cv,))
                fr["cstage"][buf] = t_cv
                return t_cv
            t_sp = dma("sync", sc_xc[dst_kind, half, :, :], cstage[buf][:, :],
                       deps=(last_cv,))
            fr["cstage"][buf] = t_sp
            return t_sp

        t_B = conv_chunk(NH, "B", 0)
        t_C = conv_chunk(NH + 1, "C", 1)
        for h in range(NH):
            conv_chunk(h, h, h % 2)

        zstate = {"zh": -1, "banks": None, "first": None, "toks": None, "tw": None,
                  "buf": 0}

        def z_open(zh):
            zstate["zh"] = zh
            zstate["buf"] = zh % 2
            zstate["tw"] = dma("gpsimd", winb[zstate["buf"]][:, :], win_z[zh, :, :],
                               deps=(fr["winb"][zstate["buf"]],))
            zstate["banks"] = [pool.acquire(pin=True) for _ in range(NSUB)]
            zstate["first"] = [True] * NSUB
            zstate["toks"] = [None] * NSUB

        def z_slice(kcs):
            buf = zstate["buf"]
            for sub in range(NSUB):
                bk = zstate["banks"][sub]
                for kc in kcs:
                    st = zstate["first"][sub]
                    zstate["first"][sub] = False
                    gd = (pool.guard_tok(bk),) if st else ()
                    sp = (kc == KC - 1)
                    zstate["toks"][sub] = b.op(
                        "tensor",
                        lambda e, kc=kc, sub=sub, st=st, sp=sp, bk=bk, buf=buf:
                        e.matmul(pool.f32[bk][:, 0:512],
                                 winb[buf][:, kc * 128:(kc + 1) * 128],
                                 xsl(kc, 3 + sub * 512, 3 + (sub + 1) * 512),
                                 start=st, stop=sp),
                        deps=(zstate["tw"],) + t_x + gd, sem=pool.wr[bk])

        def z_close():
            zh, buf = zstate["zh"], zstate["buf"]
            last = None
            for sub in range(NSUB):
                bk = zstate["banks"][sub]
                last = read_bank("scalar", bk,
                                 lambda e, bk=bk, buf=buf, sub=sub: e.activation(
                                     cstage[buf][:, sub * 512:(sub + 1) * 512],
                                     pool.f32[bk][:, 0:512], AF.Copy),
                                 deps=(zstate["toks"][sub], fr["cstage"][buf]))
                pool.unpin(bk)
            fr["winb"][buf] = zstate["toks"][NSUB - 1]
            t_sp = dma("sync", sc_zs[zh, half, :, :], cstage[buf][:, :], deps=(last,))
            fr["cstage"][buf] = t_sp
            return zstate["toks"][NSUB - 1]

        for c in range(NCH):
            bk, tk = mm1(lambda bk: pool.f32[bk][:, 0:128],
                         lambda c=c: bfm_s[:, c * L:(c + 1) * L],
                         lambda c=c: cfm_s[:, c * L:(c + 1) * L],
                         deps=(t_B, t_C))
            read_bank("vector", bk,
                      lambda e, bk=bk, c=c: e.tensor_tensor(
                          gm_s[:, c * L:(c + 1) * L], pool.f32[bk][:, 0:128],
                          tri_b[:, :], ALU.mult),
                      deps=(tk, t_trib))
            sl = c % 2
            bk7, tok7 = transpose_bf(lambda c=c: bfm_s[:, c * L:(c + 1) * L], 128,
                                     sl, deps=(t_B, t_eyeb))
            read_bank("vector", bk7,
                      lambda e, c=c, sl=sl: e.tensor_copy(
                          bt_s[:, c * L:(c + 1) * L], pool.bf[sl][:, 0:128]),
                      deps=(tok7,))
        t_pre = b.tok(EV)

        ZK = (KC + NCH - 1) // NCH

        hload = {}

        def ssm_load(h):
            buf = h % 2
            ldeps = (fr["lam"][buf], t_lamset, t_lamsp)
            tLl = dma("sync", lamL[buf][1:2, :], sc_lam[half, h:h + 1, :], deps=ldeps)
            tRr = dma("sync", lamR[buf][0:1, :], sc_lam[half, h:h + 1, :], deps=ldeps)
            tX = dma("sync", xcvh[buf][:, :], sc_xc[h, half, :, :],
                     deps=(fr["xcvh"][buf],))
            hload[h] = (tLl, tRr, tX)

        def emit_ssm(h):
            buf = h % 2
            hp = h % 2
            tLl, tRr, tX = hload.pop(h)
            pf = fr.get(("ssmstage", hp)) or ()
            tE = sop(lambda e, buf=buf: e.activation(erow[buf][:, :],
                                                     lamR[buf][0:1, :], AF.Exp),
                     deps=(tRr, fr["erow"][buf]))

            tr_toks = [None, None]
            for c in range(NCH):
                sl = c // (NCH // 2)
                bk7 = 6 + sl
                gd = (pool.guard_tok(bk7),) if c % (NCH // 2) == 0 else ()
                tr_toks[sl] = b.op(
                    "tensor",
                    lambda e, c=c, buf=buf, sl=sl: e.transpose(
                        pool.bf[sl][:, (c % (NCH // 2)) * 128:
                                     (c % (NCH // 2) + 1) * 128],
                        xcvh[buf][:, c * L:(c + 1) * L], eye_b[:, :]),
                    deps=(tX, t_eyeb) + gd, sem=pool.wr[bk7])

            eb_banks = []
            for q in range(NCH // 4):
                bkE = pool.acquire()
                tkE = b.op("tensor",
                           lambda e, q=q, buf=buf, bk=bkE: e.matmul(
                               pool.f32[bk][:, 0:512], ones_b[:, :],
                               erow[buf][0:1, q * 512:(q + 1) * 512],
                               start=True, stop=True),
                           deps=(tE, pool.guard_tok(bkE)), sem=pool.wr[bkE])
                eb_banks.append((bkE, tkE))
            dst_banks = []
            for q in range(NCH // 4):
                bkT = pool.acquire()
                tkT = None
                gd = (pool.guard_tok(bkT),)
                for j in range(4):
                    c = q * 4 + j
                    tkT = b.op("tensor",
                               lambda e, c=c, j=j, buf=buf, bk=bkT: e.matmul(
                                   pool.f32[bk][:, j * 128:(j + 1) * 128],
                                   lamL[buf][:, c * L:(c + 1) * L],
                                   lamR[buf][:, c * L:(c + 1) * L],
                                   start=True, stop=True),
                               deps=(tLl, tRr) + (gd if j == 0 else ()),
                               sem=pool.wr[bkT])
                dst_banks.append((bkT, tkT))

            tdx = None
            for c in range(NCH):
                sl = c // (NCH // 2)
                tdx = read_bank("scalar", 6 + sl,
                                lambda e, c=c, h=h, sl=sl, hp=hp: e.activation(
                                    dtxB_s[hp][:, c * L:(c + 1) * L],
                                    pool.bf[sl][:, (c % (NCH // 2)) * 128:
                                                 (c % (NCH // 2) + 1) * 128],
                                    AF.Identity,
                                    scale=dtT_s[:, c * NH + h: c * NH + h + 1]),
                                deps=(tr_toks[sl], t_dtT, t_lamsp) + pf)
            tM = None
            for q in range(NCH // 4):
                bkT, tkT = dst_banks[q]
                tRl = read_bank("scalar", bkT,
                                lambda e, bk=bkT, q=q: e.activation(
                                    mD_s[:, q * 512:(q + 1) * 512],
                                    pool.f32[bk][:, 0:512], AF.Relu, scale=-1.0),
                                deps=(tkT,) + pf + tuple(fr.get("mdfree") or ()))
                tM = sop(lambda e, q=q: e.activation(
                    mD_s[:, q * 512:(q + 1) * 512], mD_s[:, q * 512:(q + 1) * 512],
                    AF.Exp, scale=-1.0), deps=(tRl,))

            tWm = None
            for q in range(NCH // 4):
                tWm = vop(lambda e, q=q, hp=hp: e.tensor_tensor(
                    wmB_s[hp][:, q * 512:(q + 1) * 512],
                    mD_s[:, q * 512:(q + 1) * 512],
                    gm_s[:, q * 512:(q + 1) * 512], ALU.mult),
                    deps=(tM, t_pre) + pf)
            tCd = None
            tDc = None
            for q in range(NCH // 4):
                bkE, tkE = eb_banks[q]
                tCd = read_bank("vector", bkE,
                                lambda e, bk=bkE, q=q, hp=hp: e.tensor_tensor(
                                    cdB_s[hp][:, q * 512:(q + 1) * 512],
                                    cfm_s[:, q * 512:(q + 1) * 512],
                                    pool.f32[bk][:, 0:512], ALU.mult),
                                deps=(tkE, t_C) + pf)
                tDc = read_bank("vector", bkE,
                                lambda e, bk=bkE, q=q, buf=buf: e.tensor_copy(
                                    dcols[buf][:, q * 4:(q + 1) * 4],
                                    pool.f32[bk][:, 127:512:128]),
                                deps=(tkE, fr["dcp"][buf]) + pf)
            tBF = None
            for c in range(NCH):
                tBF = vop(lambda e, c=c, hp=hp: e.tensor_scalar(
                    bfB_s[hp][:, c * L:(c + 1) * L], bt_s[:, c * L:(c + 1) * L],
                    mD_s[:, c * L + 127: c * L + 128], None, ALU.mult),
                    deps=(tM, t_pre) + pf)
            fr["mdfree"] = (tWm, tBF)

            last_cd2 = last_yp = None
            for c in range(NCH):
                p = c % 2
                bkS, tkS = mm1(lambda bk: pool.f32[bk][:, 0:128],
                               lambda c=c, hp=hp: bfB_s[hp][:, c * L:(c + 1) * L],
                               lambda c=c, hp=hp: dtxB_s[hp][:, c * L:(c + 1) * L],
                               deps=(tBF, tdx))
                last_cd2 = vop(lambda e, c=c, h=h, buf=buf, hp=hp: e.tensor_scalar(
                    cd2strip[buf][:, c * L:(c + 1) * L],
                    cdB_s[hp][:, c * L:(c + 1) * L],
                    dprefst[:, h:h + 1], None, ALU.mult),
                    deps=(tCd, fr["cdsp"]))
                vop(lambda e, c=c, h=h, buf=buf: e.tensor_tensor(
                    dprefst[:, h:h + 1], dprefst[:, h:h + 1],
                    dcols[buf][:, c:c + 1], ALU.mult), deps=(last_cd2, tDc))
                thb = vop(lambda e, h=h, p=p: e.tensor_copy(
                    hb_s[p][:, :], hstate[:, h * 128:(h + 1) * 128]),
                    deps=(hu_tok[h], fr.get(("hb", p))))
                hb_tok[h] = thb
                tHU = read_bank("vector", bkS,
                                lambda e, c=c, h=h, buf=buf, bk=bkS:
                                e.scalar_tensor_tensor(
                                    hstate[:, h * 128:(h + 1) * 128],
                                    hstate[:, h * 128:(h + 1) * 128],
                                    dcols[buf][:, c:c + 1],
                                    pool.f32[bk][:, 0:128], ALU.mult, ALU.add),
                                deps=(tkS, tDc, thb))
                hu_tok[h] = tHU
                z_fill(c)
                items = [((lambda c=c, hp=hp: dtxB_s[hp][:, c * L:(c + 1) * L]),
                          (lambda c=c, hp=hp: wmB_s[hp][:, c * L:(c + 1) * L]),
                          0, 128),
                         ((lambda p=p: hb_s[p][:, :]),
                          (lambda c=c, hp=hp: cdB_s[hp][:, c * L:(c + 1) * L]),
                          0, 128)]
                bkY, tkY = mm_group(items, deps=(tdx, tWm, thb, tCd))
                fr[("hb", p)] = tkY
                last_yp = read_bank("vector", bkY,
                                    lambda e, c=c, buf=buf, bk=bkY:
                                    e.scalar_tensor_tensor(
                                        ypstrip[buf][:, c * L:(c + 1) * L],
                                        xcvh[buf][:, c * L:(c + 1) * L],
                                        dall128[:, h:h + 1],
                                        pool.f32[bk][:, 0:128], ALU.mult, ALU.add),
                                    deps=(tkY, t_dall, tX, fr["ypsp"]))
            tsp1 = dma("sync", sc_yp[h, half, :, :], ypstrip[buf][:, :],
                       deps=(last_yp,))
            tsp2 = dma("sync", sc_cd2[h, half, :, :], cd2strip[buf][:, :],
                       deps=(last_cd2,))
            last_yp_sp[h] = tsp1
            last_cd_sp[h] = tsp2
            fr["ypsp"] = tsp1
            fr["cdsp"] = tsp2
            fr["xcvh"][buf] = last_yp
            fr["lam"][buf] = b.tok(EV)
            fr["erow"][buf] = b.tok(EV)
            fr["dcp"][buf] = b.tok(EV)
            fr[("ssmstage", hp)] = (b.tok(EV),)

        zpos = [0]

        def z_fill(c):
            lo = zpos[0]
            hi = min(KC, lo + ZK)
            if lo < hi:
                z_slice(range(lo, hi))
                zpos[0] = hi

        ssm_load(0)
        for h in range(NH):
            z_open(h)
            zpos[0] = 0
            if h + 1 < NH:
                ssm_load(h + 1)
            emit_ssm(h)
            last_z = z_close()
        fr["xb"] = last_z
        fr["dtpass"] = b.tok(EV)

    phase1_half(0)
    phase1_half(1)

    cc_w = []
    for h in range(NH):
        cc_w.append(dma("sync", cc1_in[h * 128:(h + 1) * 128, :],
                        hstate[:, h * 128:(h + 1) * 128], deps=(hu_tok[h],)))
    vop(lambda e: e.memset(dtot_row[:, :], 0.0))
    tdr = vop(lambda e: e.tensor_copy(dtot_row[0:1, 0:NH], dprefst[0:1, 0:NH]),
              deps=(b.tok(EV),))
    cc_w.append(dma("sync", cc1_in[NH * 128:NH * 128 + 1, :], dtot_row[:, :],
                    deps=(tdr,)))
    t_cc1 = b.op("gpsimd",
                 lambda e: e.collective_compute(
                     "AllGather", ALU.bypass,
                     replica_groups=[list(range(NCORES))],
                     ins=[cc1_in.ap()], outs=[cc1_out.ap()]),
                 deps=tuple(cc_w), sem=CC)

    barrier_all()
    cur[0] = PH
    ygn = salloc("ygn", [128, NH * TH], BF16)
    WOW = min(2048, NH * 128)
    NKG = (NH * 128) // WOW
    woutb = [salloc(f"woutb{i}", [128, WOW], BF16) for i in range(2)]
    ypt = [salloc(f"ypt{i}", [128, TH], BF16) for i in range(2)]
    zst = [salloc(f"zst{i}", [128, TH], BF16) for i in range(2)]
    cdt = [salloc(f"cdt{i}", [128, TH], BF16) for i in range(2)]
    sjt = [salloc(f"sjt{i}", [128, 128], F32) for i in range(2)]
    hinf = salloc("hinf", [128, 128], F32)
    hint = [salloc(f"hint{i}", [128, 128], BF16) for i in range(2)]
    dscol_s = salloc("dscol_s", [128, 8], F32)
    dallt = salloc("dallt", [NH, 8], F32)
    ek_s = salloc("ek_s", [NH, 8], F32)
    dsm_s = salloc("dsm_s", [NH, 8], F32)
    dsrow = salloc("dsrow", [1, NH * 8], F32)
    xres = [salloc(f"xres{i}", [128, TH], F32) for i in range(2)]
    x1st = [salloc(f"x1st{i}", [128, 512], F32) for i in range(2)]
    x1bt = [salloc(f"x1bt{i}", [128, 512], BF16) for i in range(2)]
    sqt = [salloc(f"sqt{i}", [128, 512], BF16) for i in range(2)]
    ygt = salloc("ygt", [128, 512], BF16)
    e1t = salloc("e1t", [128, 512], F32)
    rmsB = [salloc(f"rmsB{i}", [128, 512], F32) for i in range(NSUB)]
    assert cur[0] <= SB_BYTES

    slot_free = [None, None]
    dal_toks = []
    for j in range(NCORES):
        dal_toks.append(dma("sync", dallt[:, j:j + 1],
                            cc1_out[j * (NH * 128 + 1) + NH * 128:
                                    j * (NH * 128 + 1) + NH * 128 + 1, 0:NH],
                            deps=(t_cc1,)))
    tE1 = None
    for j in range(NCORES):
        tE1 = vop(lambda e, j=j: e.scalar_tensor_tensor(
            ek_s[0:NH, j:j + 1], dallt[0:NH, j:j + 1], -1.0, cmask_s[0:NH, j:j + 1],
            ALU.add, ALU.mult), deps=(dal_toks[j], t_cm))
        tE1 = vop(lambda e, j=j: e.tensor_scalar(
            ek_s[0:NH, j:j + 1], ek_s[0:NH, j:j + 1], 1.0, None, ALU.add),
            deps=(tE1,))
    tds = vop(lambda e: e.memset(dsm_s[0:NH, :], 1.0), deps=(tE1,))
    for j in range(NCORES - 2, -1, -1):
        tds = vop(lambda e, j=j: e.tensor_tensor(
            dsm_s[0:NH, j:j + 1], dsm_s[0:NH, j + 1:j + 2], ek_s[0:NH, j + 1:j + 2],
            ALU.mult), deps=(tds,))
    for j in range(NCORES):
        tds = vop(lambda e, j=j: e.tensor_tensor(
            dsm_s[0:NH, j:j + 1], dsm_s[0:NH, j:j + 1], cmask_s[0:NH, j:j + 1],
            ALU.mult), deps=(tds,))
    t_dsw = dma("sync", sc_ds[0:NH, :], dsm_s[0:NH, :], deps=(tds,))
    t_dsr = dma("sync", dsrow[0:1, :],
                sc_ds.ap().rearrange("h j -> (h j)")[None, :], deps=(t_dsw,))
    hin_state = {"last": None, "slot": slot_free, "toks": [None] * NH,
                 "stage": [None, None], "dsc": None}

    def emit_hin(h):
        hb2 = h % 2
        bkB, tkB = mm1(lambda bk: pool.f32[bk][:, 0:8],
                       lambda: ones_f[:, :],
                       lambda h=h: dsrow[0:1, h * 8: h * 8 + 8],
                       deps=(t_dsr,))
        tds2 = read_bank("vector", bkB,
                         lambda e, bk=bkB: e.tensor_copy(dscol_s[:, :],
                                                         pool.f32[bk][:, 0:8]),
                         deps=(tkB, hin_state["dsc"]))
        tz = vop(lambda e: e.memset(hinf[:, :], 0.0),
                 deps=(hin_state["last"],) if hin_state["last"] else ())
        for j in range(NCORES):
            tS = dma("sync", sjt[j % 2][:, :],
                     cc1_out[j * (NH * 128 + 1) + h * 128:
                             j * (NH * 128 + 1) + (h + 1) * 128, :],
                     deps=(t_cc1, hin_state["slot"][j % 2]))
            tz = vop(lambda e, j=j: e.scalar_tensor_tensor(
                hinf[:, :], sjt[j % 2][:, :], dscol_s[:, j:j + 1],
                hinf[:, :], ALU.mult, ALU.add),
                deps=(tds2, tS, tz))
            hin_state["slot"][j % 2] = tz
        hin_state["dsc"] = tz
        tok = sop(lambda e, hb2=hb2: e.activation(
            hint[hb2][:, :], hinf[:, :], AF.Copy),
            deps=(tz, hin_state["stage"][hb2]))
        tok = dma("sync", sc_hin[h, :, :], hint[hb2][:, :], deps=(tok,))
        hin_state["stage"][hb2] = tok
        hin_state["last"] = tok
        hin_state["toks"][h] = tok
        return tok

    fr2 = {"yp": [None, None], "zs": [None, None], "cd": [None, None],
           "wout": [None, None], "x1": [None, None], "ygn": None,
           "xres": [None, None], "hin": [None, None]}

    for half in range(2):
        rms_g = [AccGroup(M=1, N=512) for _ in range(NSUB)]
        strips_tok = None
        hl2 = {}

        def heads_load(h):
            buf = h % 2
            if half == 0:
                t_hin_h = emit_hin(h)
            else:
                t_hin_h = hin_state["toks"][h]
            tH = dma("sync", hint[buf][:, :], sc_hin[h, :, :],
                     deps=(t_hin_h, fr2["hin"][buf]))
            tY = dma("sync", ypt[buf][:, :], sc_yp[h, half, :, :],
                     deps=(fr2["yp"][buf], last_yp_sp[h]))
            tZ = dma("sync", zst[buf][:, :], sc_zs[h, half, :, :],
                     deps=(fr2["zs"][buf],))
            tC = dma("sync", cdt[buf][:, :], sc_cd2[h, half, :, :],
                     deps=(fr2["cd"][buf], last_cd_sp[h]))
            hl2[h] = (tH, tY, tZ, tC)

        heads_load(0)
        for h in range(NH):
            buf = h % 2
            if h + 1 < NH:
                heads_load(h + 1)
            tH, tY, tZ, tC = hl2.pop(h)
            tZ = sop(lambda e, buf=buf: e.activation(
                zst[buf][:, :], zst[buf][:, :], AF.Silu), deps=(tZ,))
            tkY = None
            for sub in range(NSUB):
                bkY, tkY = mm1(lambda bk: pool.f32[bk][:, 0:512],
                               lambda buf=buf: hint[buf][:, :],
                               lambda buf=buf, sub=sub:
                               cdt[buf][:, sub * 512:(sub + 1) * 512],
                               deps=(tH, tC))
                te1 = read_bank("vector", bkY,
                                lambda e, bk=bkY, buf=buf, sub=sub: e.tensor_tensor(
                                    e1t[:, :], pool.f32[bk][:, 0:512],
                                    ypt[buf][:, sub * 512:(sub + 1) * 512], ALU.add),
                                deps=(tkY, tY))
                tyg = vop(lambda e, buf=buf, sub=sub: e.tensor_tensor(
                    ygt[:, :], e1t[:, :], zst[buf][:, sub * 512:(sub + 1) * 512],
                    ALU.mult), deps=(te1, tZ))
                tsq = vop(lambda e, sub=sub: e.tensor_tensor(
                    sqt[sub % 2][:, :], ygt[:, :], ygt[:, :], ALU.mult), deps=(tyg,))
                strips_tok = vop(lambda e, h=h, sub=sub: e.tensor_scalar(
                    ygn[:, h * TH + sub * 512: h * TH + (sub + 1) * 512], ygt[:, :],
                    nw_s[:, h:h + 1], None, ALU.mult),
                    deps=(tyg, t_nw, fr2["ygn"]))
                rms_g[sub].add(lambda sub=sub: sqt[sub % 2][:, :],
                               lambda: onescol_b[:, :], deps=(tsq,),
                               last=(h == NH - 1))
            fr2["yp"][buf] = te1
            fr2["zs"][buf] = tsq
            fr2["cd"][buf] = tkY
            fr2["hin"][buf] = tkY
        rms_toks = []
        for sub in range(NSUB):
            g = rms_g[sub]
            lo = half * TH + sub * 512
            tr = read_bank("scalar", g.bk,
                           lambda e, bk=g.bk, lo=lo: e.activation(
                               rmsr[0:1, lo:lo + 512], pool.f32[bk][0:1, 0:512],
                               AF.Ln, bias=cst_s[0:1, 1:2], scale=1.0 / cfg.DSSM),
                           deps=(g.tok,))
            tr2 = sop(lambda e, lo=lo: e.activation(
                rmsr[0:1, lo:lo + 512], rmsr[0:1, lo:lo + 512], AF.Exp, scale=-0.5),
                deps=(tr,))
            g.close()
            bkB, tkB = mm1(lambda bk: pool.f32[bk][:, 0:512],
                           lambda: ones_f[:, :],
                           lambda lo=lo: rmsr[0:1, lo:lo + 512],
                           deps=(tr2,))
            rms_toks.append(read_bank("vector", bkB,
                                      lambda e, bk=bkB, sub=sub: e.tensor_copy(
                                          rmsB[sub][:, :], pool.f32[bk][:, 0:512]),
                                      deps=(tkB,)))
        ln_g = [AccGroup(M=1, N=512) for _ in range(2 * NSUB)]
        stats_fr = {}
        GK = WOW // 128
        for m in range(KC):
            bufm = m % 2
            tXR = dma("sync", xres[bufm][:, :],
                      xt[m * 128:(m + 1) * 128, 3 + half * TH: 3 + half * TH + TH],
                      deps=(fr2["xres"][bufm],))
            acc = [AccGroup(M=128, N=512) for _ in range(NSUB)]
            for g in range(NKG):
                tw = dma("gpsimd", woutb[g % 2][:, :],
                         woutd[m, :, g * WOW:(g + 1) * WOW],
                         deps=(fr2["wout"][g % 2],))
                for sub in range(NSUB):
                    for kk in range(GK):
                        kc = g * GK + kk
                        tk = acc[sub].add(
                            (lambda kc=kc, sub=sub: ygn[:, kc * TH + sub * 512:
                                                        kc * TH + (sub + 1) * 512]),
                            (lambda g=g, kk=kk: woutb[g % 2][:, kk * 128:
                                                             (kk + 1) * 128]),
                            deps=(tw, strips_tok),
                            last=(g == NKG - 1 and kk == GK - 1))
                fr2["wout"][g % 2] = tk
            last_pe = tk
            for sub in range(NSUB):
                bk = acc[sub].bk
                tk = acc[sub].tok
                te = read_bank("vector", bk,
                               lambda e, bk=bk, sub=sub: e.tensor_tensor(
                                   e1t[:, :], pool.f32[bk][:, 0:512],
                                   rmsB[sub][:, :], ALU.mult),
                               deps=(tk, rms_toks[sub]))
                tx1 = vop(lambda e, bufm=bufm, sub=sub: e.tensor_tensor(
                    x1st[bufm][:, :], e1t[:, :],
                    xres[bufm][:, sub * 512:(sub + 1) * 512], ALU.add),
                    deps=(te, tXR, fr2["x1"][bufm]))
                tspx = dma("sync", sc_x1[m, half, :, sub * 512:(sub + 1) * 512],
                           x1st[bufm][:, :], deps=(tx1,))
                fr2["x1"][bufm] = tspx
                tb = vop(lambda e, bufm=bufm: e.tensor_copy(x1bt[bufm][:, :],
                                                            x1st[bufm][:, :]),
                         deps=(tx1, stats_fr.get(("x1b", bufm))))
                tq = vop(lambda e, bufm=bufm, sub=sub: e.tensor_tensor(
                    sqt[sub % 2][:, :], x1bt[bufm][:, :], x1bt[bufm][:, :],
                    ALU.mult), deps=(tb, stats_fr.get(("sq", sub % 2))))
                stats_fr[("x1b", bufm)] = ln_g[sub * 2].add(
                    lambda bufm=bufm: x1bt[bufm][:, :],
                    lambda: onescol_b[:, :], deps=(tb,), last=(m == KC - 1))
                stats_fr[("sq", sub % 2)] = ln_g[sub * 2 + 1].add(
                    lambda sub=sub: sqt[sub % 2][:, :],
                    lambda: onescol_b[:, :], deps=(tq,), last=(m == KC - 1))
                acc[sub].close()
            fr2["xres"][bufm] = tx1
        fr2["ygn"] = last_pe
        for sub in range(NSUB):
            lo = half * TH + sub * 512
            gs, gq = ln_g[sub * 2], ln_g[sub * 2 + 1]
            tmean = read_bank("vector", gs.bk,
                              lambda e, bk=gs.bk, lo=lo: e.tensor_scalar(
                                  mean1r[0:1, lo:lo + 512], pool.f32[bk][0:1, 0:512],
                                  1.0 / cfg.D, None, ALU.mult),
                              deps=(gs.tok,))
            tm2 = vop(lambda e, lo=lo: e.tensor_tensor(
                rstd1r[0:1, lo:lo + 512], mean1r[0:1, lo:lo + 512],
                mean1r[0:1, lo:lo + 512], ALU.mult), deps=(tmean,))
            tvar = read_bank("vector", gq.bk,
                             lambda e, bk=gq.bk, lo=lo: e.scalar_tensor_tensor(
                                 rstd1r[0:1, lo:lo + 512], pool.f32[bk][0:1, 0:512],
                                 1.0 / cfg.D, rstd1r[0:1, lo:lo + 512],
                                 ALU.mult, ALU.subtract),
                             deps=(gq.tok, tm2))
            tl = sop(lambda e, lo=lo: e.activation(
                rstd1r[0:1, lo:lo + 512], rstd1r[0:1, lo:lo + 512], AF.Ln,
                bias=cst_s[0:1, 1:2]), deps=(tvar,))
            sop(lambda e, lo=lo: e.activation(
                rstd1r[0:1, lo:lo + 512], rstd1r[0:1, lo:lo + 512], AF.Exp,
                scale=-0.5), deps=(tl,))
            gs.close()
            gq.close()
    t_ln1 = b.tok(AC)

    barrier_all()
    cur[0] = PH
    wbm_s = salloc("wbm_s", [RANK, cfg.D], BF16)
    lim_s = salloc("lim_s", [NE, RANK], BF16)
    limt_s = salloc("limt_s", [RANK, NE], BF16)
    w8t_s = salloc("w8t_s", [128, KC * NE], BF16)
    wa_s = salloc("wa_s", [128, KC * RANK], BF16)
    t_wbm = dma("gpsimd", wbm_s[:, :], wbmd[:, :])
    t_lim = dma("gpsimd", lim_s[:, :], limd[:, :])
    t_limt = dma("gpsimd", limt_s[:, :], limtd[:, :])
    t_w8t = dma("gpsimd", w8t_s[:, :], w8td[:, :])
    t_wa = dma("gpsimd", wa_s[:, :], wad[:, :])
    x1bres = salloc("x1bres", [128, KC * TH], BF16)
    wffb = [salloc(f"wffb{i}", [128, KC * 128], BF16) for i in range(2)]
    x1pt = [salloc(f"x1pt{i}", [128, TH], F32) for i in range(2)]
    zr_f = salloc("zr_f", [RANK, T], F32)
    zrb = salloc("zrb", [RANK, T], BF16)
    zp_s = salloc("zp_s", [RANK, T], BF16)
    wfull = salloc("wfull", [NE, T], BF16)
    hsa = salloc("hsa", [NE, T // 2], F32)
    dsa = salloc("dsa", [NE, T // 2], F32)
    mloc = salloc("mloc", [8, 128], F32)
    mg_s = salloc("mg_s", [1, 16], F32)
    scal_s = salloc("scal_s", [NE, 2], F32)
    lgT = salloc("lgT", [128, NE], F32)
    prb = salloc("prb", [128, NE], F32)
    msk = salloc("msk", [128, NE], F32)
    rcol = salloc("rcol", [128, 4], F32)
    wT_s = salloc("wT_s", [128, NE], F32)
    wfm_s = salloc("wfm_s", [NE, 128], F32)
    m1B = [salloc(f"m1B{i}", [128, 512], F32) for i in range(NSUB)]
    r1B = [salloc(f"r1B{i}", [128, 512], F32) for i in range(NSUB)]
    x1ft = salloc("x1ft", [128, 512], F32)
    e2t = salloc("e2t", [128, 512], F32)
    rstage = [salloc(f"rstage{i}", [128, 512], F32) for i in range(2)]
    rbt = [salloc(f"rbt{i}", [128, 512], BF16) for i in range(2)]
    sq2t = [salloc(f"sq2t{i}", [128, 512], BF16) for i in range(2)]
    ostage = [salloc(f"ostage{i}", [128, TH], F32) for i in range(2)]
    ln2m = salloc("ln2m", [1, T], F32)
    ln2r = salloc("ln2r", [1, T], F32)
    xbt2 = [salloc(f"xbt2_{i}", [128, 512], BF16) for i in range(2)]
    assert cur[0] <= SB_BYTES, cur[0]

    t_ml = vop(lambda e: e.memset(mloc[:, :], -1e30))

    def build_ln1_bcast(half, deps):
        deps = tuple(deps)
        mt, rt = [], []
        for sub in range(NSUB):
            lo = half * TH + sub * 512
            bkM, tkM = mm1(lambda bk: pool.f32[bk][:, 0:512],
                           lambda: ones_f[:, :],
                           lambda lo=lo: mean1r[0:1, lo:lo + 512], deps=deps)
            mt.append(read_bank("vector", bkM,
                                lambda e, bk=bkM, sub=sub: e.tensor_copy(
                                    m1B[sub][:, :], pool.f32[bk][:, 0:512]),
                                deps=(tkM,)))
            bkR, tkR = mm1(lambda bk: pool.f32[bk][:, 0:512],
                           lambda: ones_f[:, :],
                           lambda lo=lo: rstd1r[0:1, lo:lo + 512], deps=deps)
            rt.append(read_bank("vector", bkR,
                                lambda e, bk=bkR, sub=sub: e.tensor_copy(
                                    r1B[sub][:, :], pool.f32[bk][:, 0:512]),
                                deps=(tkR,)))
        return mt, rt

    fr3 = {"x1p": [None, None], "x1b": [None, None], "mb": ()}
    t_max = t_ml
    zh_done = []
    for half in range(2):
        mt, rt = build_ln1_bcast(half, (t_ln1,) + fr3["mb"])
        zg = [AccGroup(M=RANK, N=512) for _ in range(NSUB)]
        hg = [AccGroup(M=NE, N=512) for _ in range(NSUB)]
        prev_pe = None
        xl = {}

        def x1_load(m):
            buf = m % 2
            xl[m] = dma("sync", x1pt[buf][:, :], sc_x1[m, half, :, :],
                        deps=(fr3["x1p"][buf],))

        x1_load(0)
        for m in range(KC):
            buf = m % 2
            if m + 1 < KC:
                x1_load(m + 1)
            tx = xl.pop(m)
            for sub in range(NSUB):
                tb = vop(lambda e, buf=buf, sub=sub: e.scalar_tensor_tensor(
                    xbt2[0][:, :], x1pt[buf][:, sub * 512:(sub + 1) * 512], 1.0,
                    m1B[sub][:, :], ALU.mult, ALU.subtract),
                    deps=(tx, mt[sub], fr3["x1b"][0], prev_pe))
                tb = vop(lambda e, sub=sub: e.tensor_tensor(
                    xbt2[0][:, :], xbt2[0][:, :], r1B[sub][:, :], ALU.mult),
                    deps=(tb, rt[sub]))
                tsp = dma("sync", sc_x1b[m, half, :, sub * 512:(sub + 1) * 512],
                          xbt2[0][:, :], deps=(tb,))
                fr3["x1b"][0] = tsp
                prev_pe = zg[sub].add(lambda: xbt2[0][:, :],
                                      lambda m=m: wa_s[:, m * RANK:(m + 1) * RANK],
                                      deps=(tb, t_wa), last=(m == KC - 1))
                prev_pe = hg[sub].add(lambda: xbt2[0][:, :],
                                      lambda m=m: w8t_s[:, m * NE:(m + 1) * NE],
                                      deps=(tb, t_w8t), last=(m == KC - 1))
            fr3["x1p"][buf] = tb
        for sub in range(NSUB):
            lo = half * TH + sub * 512
            tz1 = read_bank("vector", zg[sub].bk,
                            lambda e, bk=zg[sub].bk, lo=lo: e.tensor_copy(
                                zr_f[0:RANK, lo:lo + 512],
                                pool.f32[bk][0:RANK, 0:512]),
                            deps=(zg[sub].tok,))
            tz2 = vop(lambda e, lo=lo: e.tensor_copy(
                zrb[0:RANK, lo:lo + 512], zr_f[0:RANK, lo:lo + 512]), deps=(tz1,))
            th1 = read_bank("vector", hg[sub].bk,
                            lambda e, bk=hg[sub].bk, lo=lo: e.tensor_copy(
                                hsa[0:NE, lo // 2: lo // 2 + 256],
                                pool.f32[bk][0:NE, 1:512:2]),
                            deps=(hg[sub].tok,))
            zg[sub].close()
            hg[sub].close()
            bkD, tkD = mm1(lambda bk: pool.f32[bk][0:NE, 0:256],
                           lambda: limt_s[:, :],
                           lambda lo=lo: zrb[0:RANK, lo + 1: lo + 512: 2],
                           deps=(tz2, t_limt))
            td1 = read_bank("vector", bkD,
                            lambda e, bk=bkD, lo=lo: e.tensor_copy(
                                dsa[0:NE, lo // 2: lo // 2 + 256],
                                pool.f32[bk][0:NE, 0:256]),
                            deps=(tkD,))
            tm1 = vop(lambda e, lo=lo: e.reduce_max(
                mloc[0:NE, 2:3], hsa[0:NE, lo // 2: lo // 2 + 256], axis=AXX,
                apply_absolute_value=True), deps=(th1, t_ml))
            t_max = vop(lambda e: e.tensor_tensor(
                mloc[0:NE, 0:1], mloc[0:NE, 0:1], mloc[0:NE, 2:3], ALU.max),
                deps=(tm1,))
            tm2 = vop(lambda e, lo=lo: e.reduce_max(
                mloc[0:NE, 3:4], dsa[0:NE, lo // 2: lo // 2 + 256], axis=AXX,
                apply_absolute_value=True), deps=(td1,))
            t_max = vop(lambda e: e.tensor_tensor(
                mloc[0:NE, 1:2], mloc[0:NE, 1:2], mloc[0:NE, 3:4], ALU.max),
                deps=(tm2,))
        fr3["mb"] = (t_max,)

    bkT2, tkT2 = transpose_f32(lambda: mloc[0:8, :], 8, deps=(t_max, t_eye),
                               in_parts=8)
    tcm = read_bank("vector", bkT2,
                    lambda e, bk=bkT2: e.reduce_max(rcol[:, 0:1],
                                                    pool.f32[bk][:, 0:8], axis=AXX),
                    deps=(tkT2,))
    tz8 = vop(lambda e: e.memset(z8_s[:, :], 0.0))
    tz8 = vop(lambda e: e.tensor_copy(z8_s[0:2, 0:1], rcol[0:2, 0:1]),
              deps=(tcm, tz8))
    tw2 = dma("sync", cc2_in[:, :], z8_s[:, :], deps=(tz8,))
    t_cc2 = b.op("gpsimd",
                 lambda e: e.collective_compute(
                     "AllGather", ALU.bypass,
                     replica_groups=[list(range(NCORES))],
                     ins=[cc2_in.ap()], outs=[cc2_out.ap()]),
                 deps=(tw2,), sem=CC)
    tmg = dma("sync", mg_s[0:1, 0:16],
              cc2_out.ap().rearrange("(j r) c -> j r c", r=8)[:, 0:2, 0:1],
              deps=(t_cc2,), slow=True)
    tgm = vop(lambda e: e.tensor_reduce(
        rcol[0:1, 0:2], mg_s[0:1, 0:16].rearrange("a (j c) -> a c j", c=2),
        AXX, ALU.max), deps=(tmg,))
    tsc = vop(lambda e: e.tensor_scalar(rcol[0:1, 0:2], rcol[0:1, 0:2], 1e-6, None,
                                        ALU.max), deps=(tgm,))
    tsc = vop(lambda e: e.reciprocal(rcol[0:1, 2:4], rcol[0:1, 0:2]), deps=(tsc,))
    tsc = vop(lambda e: e.tensor_scalar(rcol[0:1, 2:4], rcol[0:1, 2:4], 0.5, None,
                                        ALU.mult), deps=(tsc,))
    bkB2, tkB2 = mm1(lambda bk: pool.f32[bk][0:NE, 0:2],
                     lambda: ones_f[0:1, 0:NE],
                     lambda: rcol[0:1, 2:4], deps=(tsc,))
    t_scal = read_bank("vector", bkB2,
                       lambda e, bk=bkB2: e.tensor_copy(scal_s[0:NE, :],
                                                        pool.f32[bk][0:NE, 0:2]),
                       deps=(tkB2,))

    t_wf = None
    for a in range(T // 256):
        tl1 = vop(lambda e, a=a: e.tensor_scalar(
            dsa[0:NE, a * 128:(a + 1) * 128], dsa[0:NE, a * 128:(a + 1) * 128],
            scal_s[0:NE, 1:2], None, ALU.mult), deps=(t_scal,))
        tl2 = vop(lambda e, a=a: e.scalar_tensor_tensor(
            hsa[0:NE, a * 128:(a + 1) * 128], hsa[0:NE, a * 128:(a + 1) * 128],
            scal_s[0:NE, 0:1], dsa[0:NE, a * 128:(a + 1) * 128], ALU.mult, ALU.add),
            deps=(tl1,))
        bkt, tkt = transpose_f32(lambda a=a: hsa[0:NE, a * 128:(a + 1) * 128], NE,
                                 deps=(tl2, t_eye), in_parts=NE)
        tt1 = read_bank("vector", bkt,
                        lambda e, bk=bkt: e.tensor_copy(lgT[:, 0:NE],
                                                        pool.f32[bk][:, 0:NE]),
                        deps=(tkt,))
        ts1 = vop(lambda e: e.reduce_max(rcol[:, 0:1], lgT[:, 0:NE], axis=AXX),
                  deps=(tt1,))
        ts2 = vop(lambda e: e.tensor_scalar(rcol[:, 1:2], rcol[:, 0:1], -1.0, None,
                                            ALU.mult), deps=(ts1,))
        ts3 = sop(lambda e: e.activation(prb[:, 0:NE], lgT[:, 0:NE], AF.Exp,
                                         bias=rcol[:, 1:2]), deps=(ts2,))
        ts4 = vop(lambda e: e.reduce_sum(rcol[:, 0:1], prb[:, 0:NE], axis=AXX),
                  deps=(ts3,))
        ts5 = vop(lambda e: e.reciprocal(rcol[:, 1:2], rcol[:, 0:1]), deps=(ts4,))
        ts6 = vop(lambda e: e.tensor_scalar(prb[:, 0:NE], prb[:, 0:NE],
                                            rcol[:, 1:2], None, ALU.mult),
                  deps=(ts5,))
        tt2 = vop(lambda e: e.reduce_max(rcol[:, 0:1], prb[:, 0:NE], axis=AXX),
                  deps=(ts6,))
        tt3 = vop(lambda e: e.tensor_scalar(msk[:, 0:NE], prb[:, 0:NE],
                                            rcol[:, 0:1], None, ALU.is_ge),
                  deps=(tt2,))
        tt4 = vop(lambda e: e.scalar_tensor_tensor(msk[:, 0:NE], msk[:, 0:NE], -2.0,
                                                   prb[:, 0:NE], ALU.mult, ALU.add),
                  deps=(tt3,))
        tt5 = vop(lambda e: e.reduce_max(rcol[:, 0:1], msk[:, 0:NE], axis=AXX),
                  deps=(tt4,))
        tt6 = vop(lambda e: e.tensor_scalar(rcol[:, 1:2], rcol[:, 0:1], -2.0, None,
                                            ALU.mult), deps=(tt5,))
        tt7 = sop(lambda e: e.activation(msk[:, 0:NE], prb[:, 0:NE], AF.Sigmoid,
                                         scale=2.0, bias=rcol[:, 1:2]), deps=(tt6,))
        tt8 = vop(lambda e: e.tensor_tensor(msk[:, 0:NE], prb[:, 0:NE],
                                            msk[:, 0:NE], ALU.mult), deps=(tt7,))
        tt9 = vop(lambda e: e.reduce_sum(rcol[:, 0:1], msk[:, 0:NE], axis=AXX),
                  deps=(tt8,))
        tta = vop(lambda e: e.tensor_scalar(rcol[:, 0:1], rcol[:, 0:1], 1e-9, None,
                                            ALU.add), deps=(tt9,))
        ttb = vop(lambda e: e.reciprocal(rcol[:, 1:2], rcol[:, 0:1]), deps=(tta,))
        ttc = vop(lambda e: e.tensor_scalar(wT_s[:, 0:NE], msk[:, 0:NE],
                                            rcol[:, 1:2], None, ALU.mult),
                  deps=(ttb,))
        bkw, tkw = transpose_f32(lambda: wT_s[:, 0:NE], 128, deps=(ttc,),
                                 out_parts=NE)
        ttd = read_bank("vector", bkw,
                        lambda e, bk=bkw: e.tensor_copy(wfm_s[0:NE, :],
                                                        pool.f32[bk][0:NE, 0:128]),
                        deps=(tkw,))
        t_wf = vop(lambda e, a=a: e.tensor_copy(
            wfull[0:NE, a * 256: (a + 1) * 256: 2], wfm_s[0:NE, :]), deps=(ttd,))
        t_wf = vop(lambda e, a=a: e.tensor_copy(
            wfull[0:NE, a * 256 + 1: (a + 1) * 256: 2], wfm_s[0:NE, :]),
            deps=(t_wf,))
    t_zp = None
    for q in range(T // 512):
        bkP, tkP = mm1(lambda bk: pool.f32[bk][0:RANK, 0:512],
                       lambda: lim_s[:, :],
                       lambda q=q: wfull[0:NE, q * 512:(q + 1) * 512],
                       deps=(t_wf, t_lim))
        t_zp = read_bank("vector", bkP,
                         lambda e, bk=bkP, q=q: e.scalar_tensor_tensor(
                             zp_s[0:RANK, q * 512:(q + 1) * 512],
                             zr_f[0:RANK, q * 512:(q + 1) * 512], 0.5,
                             pool.f32[bk][0:RANK, 0:512], ALU.mult, ALU.mult),
                         deps=(tkP,))

    fr4 = {"wff": [None, None], "x1p": [None, None], "rst": [None, None],
           "x1c": None, "mb": fr3["mb"]}
    for half in range(2):
        mt, rt = build_ln1_bcast(half, (t_ln1,) + tuple(
            t for t in fr4["mb"] if t))
        t_xc_l = []
        for m in range(KC):
            t_xc_l.append(dma("sync", x1bres[:, m * TH:(m + 1) * TH],
                              sc_x1b[m, half, :, :],
                              deps=(fr4["x1c"],) if m == 0 else ()))
        t_xc = tuple(t_xc_l)
        ln_g = [AccGroup(M=1, N=512) for _ in range(2 * NSUB)]
        stats2_fr = {}
        last_tv = [None, None]
        wl = {}

        def ff_load(m):
            buf = m % 2
            t_w = dma("gpsimd", wffb[buf][:, :], wffd[m, :, :],
                      deps=(fr4["wff"][buf],))
            t_xp = dma("sync", x1pt[buf][:, :], sc_x1[m, half, :, :],
                       deps=(fr4["x1p"][buf],))
            wl[m] = (t_w, t_xp)

        ff_load(0)
        for m in range(KC):
            buf = m % 2
            if m + 1 < KC:
                ff_load(m + 1)
            t_w, t_xp = wl.pop(m)
            ptok = None
            for sub in range(NSUB):
                lo = half * TH + sub * 512
                items = [((lambda kc=kc, buf=buf:
                           wffb[buf][:, kc * 128:(kc + 1) * 128]),
                          (lambda kc=kc, sub=sub:
                           x1bres[:, kc * TH + sub * 512: kc * TH + (sub + 1) * 512]),
                          0, 512) for kc in range(KC)]
                items.append(((lambda m=m: wbm_s[:, m * 128:(m + 1) * 128]),
                              (lambda lo=lo: zp_s[0:RANK, lo:lo + 512]), 0, 512))
                bk, tk = mm_group(items, deps=(t_w, t_zp, t_wbm) + t_xc)
                ptok = tk
                te = read_bank("scalar", bk,
                               lambda e, bk=bk, m=m: e.activation(
                                   e2t[:, :], pool.f32[bk][:, 0:512], AF.Identity,
                                   bias=bcol_s[:, m:m + 1]), deps=(tk, t_bc))
                tv = vop(lambda e, buf=buf, sub=sub: e.scalar_tensor_tensor(
                    x1ft[:, :], x1pt[buf][:, sub * 512:(sub + 1) * 512], 1.0,
                    m1B[sub][:, :], ALU.mult, ALU.subtract),
                    deps=(t_xp, mt[sub], fr4.get("x1ft")))
                tv = vop(lambda e, sub=sub: e.tensor_tensor(
                    x1ft[:, :], x1ft[:, :], r1B[sub][:, :], ALU.mult),
                    deps=(tv, rt[sub]))
                tR = vop(lambda e, sub=sub: e.tensor_tensor(
                    rstage[sub % 2][:, :], e2t[:, :], x1ft[:, :], ALU.add),
                    deps=(te, tv, fr4["rst"][sub % 2]))
                fr4["x1ft"] = tR
                tsp = dma("sync", sc_r[m, half, :, sub * 512:(sub + 1) * 512],
                          rstage[sub % 2][:, :], deps=(tR,))
                fr4["rst"][sub % 2] = tsp
                trb = vop(lambda e, sub=sub: e.tensor_copy(rbt[sub % 2][:, :],
                                                           rstage[sub % 2][:, :]),
                          deps=(tR, stats2_fr.get(("rb", sub % 2))))
                tq = vop(lambda e, sub=sub: e.tensor_tensor(
                    sq2t[sub % 2][:, :], rbt[sub % 2][:, :], rbt[sub % 2][:, :],
                    ALU.mult), deps=(trb, stats2_fr.get(("sq", sub % 2))))
                stats2_fr[("rb", sub % 2)] = ln_g[sub * 2].add(
                    lambda sub=sub: rbt[sub % 2][:, :],
                    lambda: onescol_b[:, :], deps=(trb,), last=(m == KC - 1))
                stats2_fr[("sq", sub % 2)] = ln_g[sub * 2 + 1].add(
                    lambda sub=sub: sq2t[sub % 2][:, :],
                    lambda: onescol_b[:, :], deps=(tq,), last=(m == KC - 1))
                last_tv[buf] = tv
            fr4["wff"][buf] = ptok
            fr4["x1p"][buf] = last_tv[buf]
        fr4["x1c"] = ptok
        for sub in range(NSUB):
            lo = half * TH + sub * 512
            gs, gq = ln_g[sub * 2], ln_g[sub * 2 + 1]
            tmean = read_bank("vector", gs.bk,
                              lambda e, bk=gs.bk, lo=lo: e.tensor_scalar(
                                  ln2m[0:1, lo:lo + 512], pool.f32[bk][0:1, 0:512],
                                  1.0 / cfg.D, None, ALU.mult),
                              deps=(gs.tok,))
            tm2 = vop(lambda e, lo=lo: e.tensor_tensor(
                ln2r[0:1, lo:lo + 512], ln2m[0:1, lo:lo + 512],
                ln2m[0:1, lo:lo + 512], ALU.mult), deps=(tmean,))
            tvar = read_bank("vector", gq.bk,
                             lambda e, bk=gq.bk, lo=lo: e.scalar_tensor_tensor(
                                 ln2r[0:1, lo:lo + 512], pool.f32[bk][0:1, 0:512],
                                 1.0 / cfg.D, ln2r[0:1, lo:lo + 512],
                                 ALU.mult, ALU.subtract),
                             deps=(gq.tok, tm2))
            tl = sop(lambda e, lo=lo: e.activation(
                ln2r[0:1, lo:lo + 512], ln2r[0:1, lo:lo + 512], AF.Ln,
                bias=cst_s[0:1, 1:2]), deps=(tvar,))
            sop(lambda e, lo=lo: e.activation(
                ln2r[0:1, lo:lo + 512], ln2r[0:1, lo:lo + 512], AF.Exp, scale=-0.5),
                deps=(tl,))
            gs.close()
            gq.close()
        fr4["mb"] = (b.tok(EV), b.tok(AC))
    t_ln2 = b.tok(AC)

    fr5 = {"x1p": [None, None], "ost": [None, None], "mb": fr4["mb"]}
    for half in range(2):
        mt, rt = [], []
        for sub in range(NSUB):
            lo = half * TH + sub * 512
            bkM, tkM = mm1(lambda bk: pool.f32[bk][:, 0:512],
                           lambda: ones_f[:, :],
                           lambda lo=lo: ln2m[0:1, lo:lo + 512],
                           deps=(t_ln2,) + tuple(t for t in fr5["mb"] if t))
            mt.append(read_bank("vector", bkM,
                                lambda e, bk=bkM, sub=sub: e.tensor_copy(
                                    m1B[sub][:, :], pool.f32[bk][:, 0:512]),
                                deps=(tkM,)))
            bkR, tkR = mm1(lambda bk: pool.f32[bk][:, 0:512],
                           lambda: ones_f[:, :],
                           lambda lo=lo: ln2r[0:1, lo:lo + 512],
                           deps=(t_ln2,) + tuple(t for t in fr5["mb"] if t))
            rt.append(read_bank("vector", bkR,
                                lambda e, bk=bkR, sub=sub: e.tensor_copy(
                                    r1B[sub][:, :], pool.f32[bk][:, 0:512]),
                                deps=(tkR,)))
        rl = {}

        def r_load(m):
            buf = m % 2
            rl[m] = dma("sync", x1pt[buf][:, :], sc_r[m, half, :, :],
                        deps=(fr5["x1p"][buf],))

        r_load(0)
        for m in range(KC):
            buf = m % 2
            if m + 1 < KC:
                r_load(m + 1)
            t_r = rl.pop(m)
            to = fr5.get("x1ft")
            for sub in range(NSUB):
                to = vop(lambda e, buf=buf, sub=sub: e.scalar_tensor_tensor(
                    x1ft[:, :], x1pt[buf][:, sub * 512:(sub + 1) * 512], 1.0,
                    m1B[sub][:, :], ALU.mult, ALU.subtract),
                    deps=(t_r, mt[sub], to))
                to = vop(lambda e, buf=buf, sub=sub: e.tensor_tensor(
                    ostage[buf][:, sub * 512:(sub + 1) * 512], x1ft[:, :],
                    r1B[sub][:, :], ALU.mult),
                    deps=(to, rt[sub], fr5["ost"][buf]))
            fr5["x1ft"] = to
            t_o = dma_out(ytd[m * 128:(m + 1) * 128, half * TH: half * TH + TH],
                          ostage[buf][:, :], deps=(to,))
            fr5["ost"][buf] = t_o
            fr5["x1p"][buf] = to
        fr5["mb"] = (b.tok(EV),)

    final_deps = tuple((s, b.cnt[s]) for q in qlanes for s in qlanes[q])
    b.op("sync", lambda e: e.nop(), deps=final_deps)

    with nc.Block() as block:
        b.finalize(block)
    b.es.close()
    return nc


def kernel(**inputs):
    cfg = FULL
    per_core = shard_inputs(cfg, inputs)
    nc = build_program(cfg)
    res = run_bass_kernel_spmd(nc, per_core, core_ids=list(range(NCORES)))
    outs = [np.ascontiguousarray(res.results[i]["yt"].T) for i in range(NCORES)]
    return np.concatenate(outs, axis=0).reshape(1, cfg.S, cfg.D).astype(np.float32)



def _ref_numpy(cfg, inp):
    x = inp["L0_attn_hidden_states"].reshape(cfg.S, cfg.D).astype(np.float64)
    W_in = inp["L0_attn_W_in"].astype(np.float64)
    conv_w = inp["L0_attn_conv_w"].astype(np.float64)
    dtb = inp["L0_attn_dt_bias"].astype(np.float64)
    alog = inp["L0_attn_A_log"].astype(np.float64)
    Dp = inp["L0_attn_D_param"].astype(np.float64)
    nw = inp["L0_attn_norm_weight"].astype(np.float64)
    W_out = inp["L0_attn_W_out"].astype(np.float64)
    Wf = inp["L0_ffn_W"].astype(np.float64)
    bf = inp["L0_ffn_b"].astype(np.float64)
    Af = inp["L0_ffn_A"].astype(np.float64)
    Bm = inp["L0_ffn_Bmat"].astype(np.float64)
    Li = inp["L0_ffn_LiMEs"].astype(np.float64)
    NH, DSSM, CONV = cfg.NH, cfg.DSSM, cfg.CONV

    def silu(v):
        return v / (1.0 + np.exp(-v))

    proj = x @ W_in
    z = proj[:, :DSSM]
    xBC = proj[:, DSSM:DSSM + CONV]
    dt = proj[:, DSSM + CONV:]
    pad = np.vstack([np.zeros((3, CONV)), xBC])
    conv = sum(pad[k:k + cfg.S] * conv_w[:, k] for k in range(4))
    xBC = silu(conv)
    xs = xBC[:, :DSSM].reshape(cfg.S, NH, 128)
    Bmv = xBC[:, DSSM:DSSM + 128]
    Cmv = xBC[:, DSSM + 128:]
    dt = np.log1p(np.exp(-(np.abs(dt + dtb[None, :])))) + np.maximum(dt + dtb[None, :], 0)
    dA = np.exp(dt * -np.exp(alog)[None, :])
    h = np.zeros((NH, 128, 128))
    ys = np.zeros((cfg.S, NH, 128))
    for t in range(cfg.S):
        h = dA[t][:, None, None] * h + (dt[t][:, None] * xs[t])[:, :, None] * Bmv[t][None, None, :]
        ys[t] = h @ Cmv[t]
    y = ys + Dp[None, :, None] * xs
    y = y.reshape(cfg.S, DSSM) * silu(z)
    var = np.mean(y * y, axis=-1, keepdims=True)
    y = y / np.sqrt(var + 1e-5) * nw[None, :]
    attn = y @ W_out

    def ln(v):
        m = v.mean(-1, keepdims=True)
        s = v.var(-1, keepdims=True)
        return (v - m) / np.sqrt(s + 1e-5)

    x1 = ln(x + attn)
    H = x1 @ Wf.T + bf
    zr = x1 @ Af
    anchors = np.arange(1, cfg.S, 2)
    Hs = H[anchors][:, :cfg.NE]
    Ds = zr[anchors] @ Li.T
    hsc = max(np.abs(Hs).max(), 1e-6)
    dsc = max(np.abs(Ds).max(), 1e-6)
    lg = 0.5 * Hs / hsc + 0.5 * Ds / dsc
    p = np.exp(lg - lg.max(-1, keepdims=True))
    p = p / p.sum(-1, keepdims=True)
    thr = np.sort(p, axis=-1)[:, -2:-1]
    mask = 1.0 / (1.0 + np.exp(-(p - thr) / 0.5))
    w = p * mask
    w = w / (w.sum(-1, keepdims=True) + 1e-9)
    wfull = np.repeat(w, 2, axis=0)
    pm = wfull @ Li
    delta = (zr * pm) @ Bm * 0.5
    out = ln(x1 + H + delta)
    return out.reshape(1, cfg.S, cfg.D)


def _gen_mini_inputs(cfg, seed=0):
    rng = np.random.default_rng(seed)
    f = lambda *s: rng.standard_normal(s).astype(np.float32)
    return {
        "L0_attn_hidden_states": f(1, cfg.S, cfg.D),
        "L0_attn_W_in": (f(cfg.D, cfg.PROJ) * 0.02).astype(np.float32),
        "L0_attn_conv_w": (f(cfg.CONV, 4) * 0.2).astype(np.float32),
        "L0_attn_dt_bias": f(cfg.NH),
        "L0_attn_A_log": np.log(rng.uniform(1, 16, cfg.NH)).astype(np.float32),
        "L0_attn_D_param": f(cfg.NH),
        "L0_attn_norm_weight": (1 + 0.1 * f(cfg.DSSM)).astype(np.float32),
        "L0_attn_W_out": (f(cfg.DSSM, cfg.D) * 0.02).astype(np.float32),
        "L0_ffn_W": (f(cfg.D, cfg.D) / np.sqrt(cfg.D)).astype(np.float32),
        "L0_ffn_b": (f(cfg.D) * 0.01).astype(np.float32),
        "L0_ffn_A": (f(cfg.D, cfg.RANK) / np.sqrt(cfg.D)).astype(np.float32),
        "L0_ffn_Bmat": (f(cfg.RANK, cfg.D) * 0.01).astype(np.float32),
        "L0_ffn_LiMEs": (f(cfg.NE, cfg.RANK) * 0.1).astype(np.float32),
    }


def selftest_mini(use_sim=True, cfg=None):
    cfg = cfg or MINI
    inp = _gen_mini_inputs(cfg)
    per_core = shard_inputs(cfg, inp)
    nc = build_program(cfg)
    if use_sim:
        from concourse import bass_interp
        sim = bass_interp.MultiCoreSim(nc, NCORES)
        for i in range(NCORES):
            for k, v in per_core[i].items():
                sim.cores[i].tensor(k)[:] = v
        sim.simulate()
        outs = [np.ascontiguousarray(np.asarray(sim.cores[i].mem_tensor("yt")).T)
                for i in range(NCORES)]
    else:
        res = run_bass_kernel_spmd(nc, per_core, core_ids=list(range(NCORES)))
        outs = [np.ascontiguousarray(res.results[i]["yt"].T) for i in range(NCORES)]
    got = np.concatenate(outs, axis=0).reshape(1, cfg.S, cfg.D)
    ref = _ref_numpy(cfg, inp)
    err = got - ref
    rv = float(np.mean(err ** 2) / np.mean(ref ** 2))
    print("mini residual-variance ratio:", rv)
    print("mini max abs err:", float(np.abs(err).max()),
          " ref scale:", float(np.abs(ref).max()))
    return rv


if __name__ == "__main__":
    import sys
    if "--mini-hw" in sys.argv:
        selftest_mini(use_sim=False)
    elif "--mid" in sys.argv:
        selftest_mini(use_sim=True, cfg=MID)
    else:
        selftest_mini(use_sim=True)


# version 18
# speedup vs baseline: 1.0125x; 1.0125x over previous
import numpy as np
from contextlib import ExitStack

import concourse.bass as bass
import concourse.mybir as mybir
from concourse.bass_utils import run_bass_kernel_spmd

F32 = mybir.dt.float32
BF16 = mybir.dt.bfloat16
AF = mybir.ActivationFunctionType
ALU = mybir.AluOpType
AXX = mybir.AxisListType.X

NCORES = 8
L = 128


class CFG:
    def __init__(self, d_model, nh, t_core, mini=False):
        self.mini = mini
        self.D = d_model
        self.KC = d_model // 128
        self.NH = nh
        self.DSSM = nh * 128
        self.CONV = self.DSSM + 256
        self.NCV = nh + 2
        self.PROJ = self.DSSM + self.CONV + nh
        self.T = t_core
        self.TH = t_core // 2
        self.NSUB = self.TH // 512
        self.NCH = self.TH // L
        self.S = t_core * NCORES
        self.RANK = 32
        self.NE = 8
        assert self.TH % 512 == 0


FULL = CFG(4096, 64, 2048)
MINI = CFG(512, 8, 1024, mini=True)
MID = CFG(512, 8, 2048, mini=True)


def _tile_lhs(w, kc, mch, mw=128):
    K, M = w.shape
    assert K == kc * 128 and M == mch * mw
    t = w.reshape(kc, 128, mch, mw)
    t = np.transpose(t, (2, 1, 0, 3))
    return np.ascontiguousarray(t.reshape(mch, 128, kc * mw), dtype=np.float32)


def shard_inputs(cfg, inp):
    x = np.asarray(inp["L0_attn_hidden_states"], dtype=np.float32).reshape(cfg.S, cfg.D)
    W_in = np.asarray(inp["L0_attn_W_in"], dtype=np.float32)
    conv_w = np.asarray(inp["L0_attn_conv_w"], dtype=np.float32)
    NH, KC, DSSM = cfg.NH, cfg.KC, cfg.DSSM
    Wf = np.asarray(inp["L0_ffn_W"], dtype=np.float32)
    base = {
        "win_z": _tile_lhs(W_in[:, :DSSM], KC, NH),
        "win_xbc": _tile_lhs(W_in[:, DSSM:DSSM + cfg.CONV], KC, cfg.NCV),
        "wdt": _tile_lhs(W_in[:, DSSM + cfg.CONV:], KC, 1, mw=NH)[0],
        "wout": _tile_lhs(np.asarray(inp["L0_attn_W_out"], dtype=np.float32), NH, KC),
        "wff": _tile_lhs(np.ascontiguousarray(Wf.T), KC, KC),
        "w8t": _tile_lhs(np.ascontiguousarray(Wf[:cfg.NE, :].T), KC, 1, mw=cfg.NE)[0],
        "wa": _tile_lhs(np.asarray(inp["L0_ffn_A"], dtype=np.float32), KC, 1,
                        mw=cfg.RANK)[0],
        "wbm": np.ascontiguousarray(np.concatenate([
            np.asarray(inp["L0_ffn_Bmat"], dtype=np.float32),
            np.zeros((128 - cfg.RANK, cfg.D), dtype=np.float32)], axis=0)),
        "lim": np.ascontiguousarray(inp["L0_ffn_LiMEs"], dtype=np.float32),
        "limt": np.ascontiguousarray(
            np.asarray(inp["L0_ffn_LiMEs"], dtype=np.float32).T),
        "wcv": np.ascontiguousarray(
            conv_w.reshape(cfg.NCV, 128, 4).transpose(1, 0, 2).reshape(128, cfg.NCV * 4),
            dtype=np.float32),
        "nw_t": np.ascontiguousarray(
            np.asarray(inp["L0_attn_norm_weight"], dtype=np.float32).reshape(NH, 128).T),
        "bcol": np.ascontiguousarray(
            np.asarray(inp["L0_ffn_b"], dtype=np.float32).reshape(KC, 128).T),
        "dtb": np.asarray(inp["L0_attn_dt_bias"], dtype=np.float32).reshape(NH, 1),
        "alog": np.asarray(inp["L0_attn_A_log"], dtype=np.float32).reshape(NH, 1),
        "dpr": np.asarray(inp["L0_attn_D_param"], dtype=np.float32).reshape(1, NH),
        "eye": np.eye(128, dtype=np.float32),
        "tri": np.ascontiguousarray(np.triu(np.ones((128, 128), dtype=np.float32))),
    }
    per_core = []
    for i in range(NCORES):
        lo = i * cfg.T
        xs = np.zeros((cfg.T + 3, cfg.D), dtype=np.float32)
        src_lo = max(0, lo - 3)
        xs[3 - (lo - src_lo):] = x[src_lo:lo + cfg.T]
        cm = np.zeros((NH, 8), dtype=np.float32)
        cm[:, :i] = 1.0
        m = dict(base)
        m["xt"] = np.ascontiguousarray(xs.T)
        m["cmask"] = cm
        per_core.append(m)
    return per_core


ENGS = ("tensor", "vector", "scalar", "sync", "gpsimd")


class B:
    def __init__(self, nc):
        self.nc = nc
        self.ops = {e: [] for e in ENGS}
        self.sems = {}
        self.cnt = {}
        self.wm = {e: {} for e in ENGS}
        self.es = ExitStack()

    def sem(self, name):
        self.sems[name] = self.es.enter_context(self.nc.semaphore(name))
        self.cnt[name] = 0
        return name

    def op(self, eng, fn, deps=(), sem=None, k=1):
        for d in deps:
            if d is None:
                continue
            s, v = d
            if v <= 0 or self.wm[eng].get(s, 0) >= v:
                continue
            self.wm[eng][s] = v
            h = self.sems[s]
            self.ops[eng].append(lambda e, h=h, v=v: e.wait_ge(h, v))
        if fn is None:
            return None
        if sem is None:
            self.ops[eng].append(fn)
            return None
        self.cnt[sem] += k
        v = self.cnt[sem]
        h = self.sems[sem]
        self.ops[eng].append(lambda e, fn=fn, h=h, k=k: fn(e).then_inc(h, k))
        return (sem, v)

    def tok(self, sem):
        return (sem, self.cnt[sem])

    def finalize(self, block):
        for eng in ENGS:
            thunks = self.ops[eng]
            if not thunks:
                continue

            def run(e, thunks=thunks):
                for fn in thunks:
                    fn(e)
            getattr(block, eng)(run)


class Pool:

    def __init__(self, b, nc, es):
        self.b = b
        self.f32 = [es.enter_context(nc.psum_tensor(f"psb{i}", [128, 512], F32))
                    for i in range(6)]
        self.bf = [es.enter_context(nc.psum_tensor(f"psbf{i}", [128, 1024], BF16))
                   for i in range(2)]
        self.rd = [b.sem(f"psr{i}") for i in range(8)]
        self.wr = [b.sem(f"psw{i}") for i in range(8)]
        self.rr = 0
        self.pinned = set()

    def acquire(self, pin=False):
        for _ in range(8):
            i = self.rr
            self.rr = (self.rr + 1) % 6
            if i not in self.pinned:
                if pin:
                    self.pinned.add(i)
                return i
        raise RuntimeError("no free bank")

    def guard_tok(self, i):
        return (self.rd[i], self.b.cnt[self.rd[i]])

    def unpin(self, i):
        self.pinned.discard(i)


def build_program(cfg: CFG):
    nc = bass.Bass("TRN2", target_bir_lowering=False)
    KC, NH, TH, NCH, NSUB = cfg.KC, cfg.NH, cfg.TH, cfg.NCH, cfg.NSUB
    T, NCV, NE, RANK = cfg.T, cfg.NCV, cfg.NE, cfg.RANK
    W = TH + 3

    def din(name, shape, dtype=F32):
        return nc.dram_tensor(name, shape, dtype, kind="ExternalInput")

    def dint(name, shape, dtype=F32):
        return nc.dram_tensor(name, shape, dtype)

    xt = din("xt", [cfg.D, T + 3])
    win_z = din("win_z", [NH, 128, KC * 128])
    win_xbc = din("win_xbc", [NCV, 128, KC * 128])
    wdtd = din("wdt", [128, KC * NH])
    woutd = din("wout", [KC, 128, NH * 128])
    wffd = din("wff", [KC, 128, KC * 128])
    w8td = din("w8t", [128, KC * NE])
    wad = din("wa", [128, KC * RANK])
    wbmd = din("wbm", [128, cfg.D])
    limd = din("lim", [NE, RANK])
    limtd = din("limt", [RANK, NE])
    wcvd = din("wcv", [128, NCV * 4])
    nwd = din("nw_t", [128, NH])
    bcold = din("bcol", [128, KC])
    dtbd = din("dtb", [NH, 1])
    alogd = din("alog", [NH, 1])
    dprd = din("dpr", [1, NH])
    cmaskd = din("cmask", [NH, 8])
    eyed = din("eye", [128, 128])
    trid = din("tri", [128, 128])
    ytd = nc.dram_tensor("yt", [cfg.D, T], F32, kind="ExternalOutput")

    sc_lam = dint("sc_lam", [2, NH, TH])
    sc_xc = dint("sc_xc", [NH, 2, 128, TH], BF16)
    sc_zs = dint("sc_zs", [NH, 2, 128, TH], BF16)
    sc_yp = dint("sc_yp", [NH, 2, 128, TH], BF16)
    sc_cd2 = dint("sc_cd2", [NH, 2, 128, TH], BF16)
    sc_x1 = dint("sc_x1", [KC, 2, 128, TH])
    sc_x1b = dint("sc_x1b", [KC, 2, 128, TH], BF16)
    sc_r = dint("sc_r", [KC, 2, 128, TH])
    sc_ds = dint("sc_ds", [NH, 8])
    sc_hin = dint("sc_hin", [NH, 128, 128], BF16)
    cc1_in = dint("cc1_in", [NH * 128 + 1, 128])
    cc1_out = dint("cc1_out", [NCORES * (NH * 128 + 1), 128])
    cc2_in = dint("cc2_in", [8, 4])
    cc2_out = dint("cc2_out", [NCORES * 8, 4])

    b = B(nc)
    es = b.es
    EV = b.sem("ev")
    AC = b.sem("ac")
    CC = b.sem("cc")
    NL = 8
    qlanes = {"sync": [b.sem(f"sl{i}") for i in range(NL)],
              "gpsimd": [b.sem(f"gl{i}") for i in range(NL)]}
    qcount = {"sync": [0], "gpsimd": [0]}
    out_dma_toks = []
    pool = Pool(b, nc, es)

    SB_BYTES = 207 * 1024
    arena = es.enter_context(nc.sbuf_tensor("arena", [128, SB_BYTES], mybir.dt.uint8))
    arena_base = nc.lookup_mloc(arena).addr
    cur = [0]

    def salloc(name, shape, dtype, at=None):
        per_part = int(np.prod(shape[1:])) * mybir.dt.size(dtype)
        if at is None:
            addr = (cur[0] + 31) // 32 * 32
            cur[0] = addr + per_part
            assert cur[0] <= SB_BYTES, f"SBUF overflow at {name}: {cur[0]}"
        else:
            addr = at
            assert addr + per_part <= SB_BYTES
        h = nc.virtual_sbuf_tensor(name, shape, dtype)
        nc._tensor(h.name, h.shape, h.dtype, type="SB")
        mloc = nc.lookup_mloc(h)
        mloc.allocated = True
        mloc.addr = arena_base + addr
        return h

    def s_offset(h):
        return nc.lookup_mloc(h).addr - arena_base

    eye_f = salloc("eye_f", [128, 128], F32)
    eye_b = salloc("eye_b", [128, 128], BF16)
    tri_b = salloc("tri_b", [128, 128], BF16)
    ones_f = salloc("ones_f", [1, 128], F32)
    ones_b = salloc("ones_b", [1, 128], BF16)
    onescol_b = salloc("onescol_b", [128, 1], BF16)
    wcv_s = salloc("wcv_s", [128, NCV * 4], F32)
    nw_s = salloc("nw_s", [128, NH], F32)
    bcol_s = salloc("bcol_s", [128, KC], F32)
    dtb_s = salloc("dtb_s", [NH, 1], F32)
    nea_s = salloc("nea_s", [NH, 1], F32)
    dpr_s = salloc("dpr_s", [1, NH], F32)
    cmask_s = salloc("cmask_s", [NH, 8], F32)
    dprefst = salloc("dprefst", [128, NH], F32)
    mean1r = salloc("mean1r", [1, T], F32)
    rstd1r = salloc("rstd1r", [1, T], F32)
    rmsr = salloc("rmsr", [1, T], F32)
    dcol128 = salloc("dcol128", [128, 2], F32)
    cst_s = salloc("cst_s", [128, 2], F32)
    dtot_row = salloc("dtot_row", [1, 128], F32)
    z8_s = salloc("z8_s", [8, 4], F32)
    PH = cur[0]

    def dma(q, dst, src, deps=(), slow=False):
        lanes = qlanes[q]
        lane = qcount[q][0] % NL
        qcount[q][0] += 1
        sem = lanes[lane]
        prev = (sem, b.cnt[sem])
        return b.op(q, lambda e, d=dst, s=src, sl=slow: e.dma_start(
            out=d, in_=s, allow_slow_non_contiguous=sl),
            deps=tuple(deps) + (prev,), sem=sem, k=16)

    def dma_out(dst, src, deps=()):
        t = dma("sync", dst, src, deps=deps)
        out_dma_toks.append(t)
        return t

    def vop(fn, deps=()):
        return b.op("vector", fn, deps=deps, sem=EV)

    def sop(fn, deps=()):
        return b.op("scalar", fn, deps=deps, sem=AC)

    def mm_group(items, deps=(), M=128):
        bk = pool.acquire()
        deps = tuple(deps) + (pool.guard_tok(bk),)
        n = len(items)
        tok = None
        for j, (lf, rf, lo, nn) in enumerate(items):
            st, sp = (j == 0), (j == n - 1)
            fn = (lambda e, lf=lf, rf=rf, lo=lo, nn=nn, st=st, sp=sp, bk=bk:
                  e.matmul(pool.f32[bk][0:M, lo:lo + nn], lf(), rf(), start=st, stop=sp))
            if sp:
                tok = b.op("tensor", fn, deps=deps if st else (), sem=pool.wr[bk])
            else:
                b.op("tensor", fn, deps=deps if st else ())
        return bk, tok

    def mm1(out_fn, lhs_fn, rhs_fn, deps=()):
        bk = pool.acquire()
        deps = tuple(deps) + (pool.guard_tok(bk),)
        tok = b.op("tensor",
                   lambda e, bk=bk: e.matmul(out_fn(bk), lhs_fn(), rhs_fn(),
                                             start=True, stop=True),
                   deps=deps, sem=pool.wr[bk])
        return bk, tok

    def read_bank(eng, bk, fn, deps=()):
        prev = (pool.rd[bk], b.cnt[pool.rd[bk]])
        return b.op(eng, fn, deps=tuple(deps) + (prev,), sem=pool.rd[bk])

    def transpose_f32(in_fn, n_cols, deps=(), in_parts=128, out_parts=128):
        bk = pool.acquire()
        deps = tuple(deps) + (pool.guard_tok(bk),)
        tok = b.op("tensor",
                   lambda e, bk=bk: e.transpose(pool.f32[bk][0:out_parts, 0:n_cols],
                                                in_fn(),
                                                eye_f[0:in_parts, 0:in_parts]),
                   deps=deps, sem=pool.wr[bk])
        return bk, tok

    def transpose_bf(in_fn, n_cols, slot, deps=()):
        bk = 6 + slot
        deps = tuple(deps) + (pool.guard_tok(bk),)
        tok = b.op("tensor",
                   lambda e, n=n_cols, slot=slot: e.transpose(
                       pool.bf[slot][:, 0:n], in_fn(), eye_b[:, :]),
                   deps=deps, sem=pool.wr[bk])
        return bk, tok

    def barrier_all():
        toks = [(s, b.cnt[s]) for s in b.sems]
        for eng in ENGS:
            b.op(eng, None, deps=toks)

    class AccGroup:

        def __init__(self, M=1, N=512):
            self.bk = pool.acquire(pin=True)
            self.M, self.N = M, N
            self.first = True
            self.tok = None

        def add(self, rhs_fn, lhs_fn, deps=(), last=False):
            gd = (pool.guard_tok(self.bk),) if self.first else ()
            st = self.first
            self.first = False
            self.tok = b.op("tensor",
                            lambda e, bk=self.bk, st=st, lf=lhs_fn, rf=rhs_fn,
                            sp=last: e.matmul(pool.f32[bk][0:self.M, 0:self.N],
                                              lf(), rf(), start=st, stop=sp),
                            deps=tuple(deps) + gd, sem=pool.wr[self.bk])
            return self.tok

        def close(self):
            pool.unpin(self.bk)

    t_eye = dma("sync", eye_f[:, :], eyed[:, :])
    t_wcv = dma("sync", wcv_s[:, :], wcvd[:, :])
    t_nw = dma("sync", nw_s[:, :], nwd[:, :])
    t_bc = dma("sync", bcol_s[:, :], bcold[:, :])
    t_dtb = dma("sync", dtb_s[:, :], dtbd[:, :])
    t_alog = dma("sync", nea_s[:, :], alogd[:, :])
    t_dpr = dma("sync", dpr_s[:, :], dprd[:, :])
    t_cm = dma("sync", cmask_s[:, :], cmaskd[:, :])
    t_eyeb = dma("gpsimd", eye_b[:, :], eyed[:, :])
    t_trib = dma("gpsimd", tri_b[:, :], trid[:, :])

    b.op("vector", lambda e: e.memset(cst_s[:, 0:1], 1.0))
    b.op("vector", lambda e: e.memset(cst_s[:, 1:2], 1e-5))
    b.op("vector", lambda e: e.memset(ones_f[:, :], 1.0))
    b.op("vector", lambda e: e.memset(ones_b[:, :], 1.0))
    b.op("vector", lambda e: e.memset(onescol_b[:, :], 1.0))
    b.op("vector", lambda e: e.memset(dprefst[:, :], 1.0))
    t_msets = vop(lambda e: e.memset(hstate[:, :], 0.0))
    t_ea = sop(lambda e: e.activation(nea_s[:, :], nea_s[:, :], AF.Exp),
               deps=(t_alog,))
    t_nea = vop(lambda e: e.tensor_scalar(nea_s[:, :], nea_s[:, :], -1.0, None,
                                          ALU.mult), deps=(t_ea,))
    t_consts = b.tok(EV)
    for _eng in ENGS:
        b.op(_eng, None, deps=(t_consts,))

    cur[0] = PH
    hstate = salloc("hstate", [128, NH * 128], F32)
    dall128 = salloc("dall128", [128, NH], F32)
    xb_s = salloc("xb_s", [128, KC * W], BF16)
    winb = [salloc(f"winb{i}", [128, KC * 128], BF16) for i in range(2)]
    convin = [salloc(f"convin{i}", [128, W], BF16) for i in range(2)]
    halo_s = salloc("halo_s", [128, NCV * 4], BF16)
    cstage = [salloc(f"cstage{i}", [128, TH], BF16) for i in range(2)]
    dt_u = salloc("dt_u", [NH, TH], F32)
    dt_t1 = salloc("dt_t1", [NH, TH], F32)
    dt_t2 = salloc("dt_t2", [NH, TH], F32)
    dts_s = dt_t2
    lam_sb = dt_u
    dtT_s = salloc("dtT_s", [128, NCH * NH], F32)
    diag_s = [salloc(f"diag{i}", [128, 128], BF16) for i in range(4)]
    bfm_s = salloc("bfm_s", [128, TH], BF16)
    cfm_s = salloc("cfm_s", [128, TH], BF16)
    bt_s = salloc("bt_s", [128, NCH * 128], BF16)
    gm_s = salloc("gm_s", [128, NCH * 128], BF16)
    lamL = [salloc(f"lamL{i}", [2, TH], F32) for i in range(2)]
    lamR = [salloc(f"lamR{i}", [2, TH], F32) for i in range(2)]
    erow = [salloc(f"erow{i}", [1, TH], BF16) for i in range(2)]
    dcols = [salloc(f"dcols{i}", [128, NCH], F32) for i in range(2)]
    xcvh = [salloc(f"xcvh{i}", [128, TH], BF16) for i in range(2)]
    ypstrip = [salloc("ypstrip0", [128, TH], BF16)] * 2
    cd2strip = [salloc("cd2strip0", [128, TH], BF16)] * 2
    mD_s = salloc("mD_s", [128, NCH * 128], F32, at=s_offset(dt_t1))
    wmB_s = salloc("wmB_s", [128, NCH * 128], BF16, at=s_offset(dt_t2))
    dtxB_s = salloc("dtxB_s", [128, NCH * 128], BF16)
    bfB_s = salloc("bfB_s", [128, NCH * 128], BF16)
    cdB_s = salloc("cdB_s", [128, NCH * 128], BF16)
    hb_s = [salloc(f"hb_s{i}", [128, 128], BF16) for i in range(2)]
    assert cur[0] <= SB_BYTES

    bkDP, tkDP = mm1(lambda bk: pool.f32[bk][:, 0:NH],
                     lambda: ones_f[:, :], lambda: dpr_s[0:1, 0:NH],
                     deps=(t_dpr,))
    t_dall = read_bank("vector", bkDP,
                       lambda e, bk=bkDP: e.tensor_copy(dall128[:, 0:NH],
                                                        pool.f32[bk][:, 0:NH]),
                       deps=(tkDP,))
    for i in range(2):
        b.op("vector", lambda e, i=i: e.memset(lamL[i][:, :], 1.0))
        b.op("vector", lambda e, i=i: e.memset(lamR[i][:, :], -1.0))
    t_lamset = vop(lambda e: e.nop())

    fr = {"winb": [None, None], "convin": [None, None], "cstage": [None, None],
          "lam": [None, None], "xcvh": [None, None], "strip": [None, None],
          "xb": None, "dtpass": None, "dtT": None, "lamsb": None,
          "conv_pe": None, "diag": None, "erow": [None, None], "ypsp": None,
          "cdsp": None, "dcp": [None, None]}
    hu_tok = [t_msets] * NH
    cc1_sp = []
    hb_tok = [None] * NH
    last_yp_sp = [None] * NH
    last_cd_sp = [None] * NH

    def xsl(kc, lo, hi):
        return xb_s[:, kc * W + lo: kc * W + hi]

    def phase1_half(half):
        t_x_l = []
        xtr = xt.ap().rearrange("(c p) w -> p c w", p=128)
        XB = 4
        for kc0 in range(0, KC, XB):
            nb = min(XB, KC - kc0)
            t_x_l.append(dma("gpsimd", xb_s[:, kc0 * W:(kc0 + nb) * W],
                             xtr[:, kc0:kc0 + nb, TH * half: TH * half + W],
                             deps=(fr["xb"],) if kc0 == 0 else ()))
        t_x = tuple(t_x_l)
        t_w = dma("gpsimd", winb[0][:, 0:KC * NH], wdtd[:, :],
                  deps=(fr["winb"][0],))
        ev_toks = []
        last_dt_tok = None
        for sub in range(NSUB):
            items = [((lambda kc=kc: winb[0][:, kc * NH:(kc + 1) * NH]),
                      (lambda kc=kc, sub=sub: xsl(kc, 3 + sub * 512, 3 + (sub + 1) * 512)),
                      0, 512) for kc in range(KC)]
            bk, tk = mm_group(items, deps=(t_w,) + t_x, M=NH)
            last_dt_tok = tk
            ev_toks.append(read_bank("scalar", bk,
                                     lambda e, bk=bk, sub=sub: e.activation(
                                         dt_u[0:NH, sub * 512:(sub + 1) * 512],
                                         pool.f32[bk][0:NH, 0:512], AF.Identity,
                                         bias=dtb_s[:, :]),
                                     deps=(tk, t_dtb, fr["dtpass"])))
        fr["winb"][0] = last_dt_tok
        t1 = sop(lambda e: e.activation(dt_t1[0:NH, :], dt_u[0:NH, :], AF.Abs),
                 deps=tuple(ev_toks) + (fr["dtpass"],))
        t2 = sop(lambda e: e.activation(dt_t2[0:NH, :], dt_t1[0:NH, :], AF.Exp,
                                        scale=-1.0), deps=(t1,))
        t3 = sop(lambda e: e.activation(dt_t2[0:NH, :], dt_t2[0:NH, :], AF.Ln,
                                        bias=cst_s[0:NH, 0:1]), deps=(t2,))
        t4 = vop(lambda e: e.tensor_scalar(dt_t1[0:NH, :], dt_u[0:NH, :], 0.0, None,
                                           ALU.max), deps=(t3,))
        t5 = vop(lambda e: e.tensor_tensor(dts_s[0:NH, :], dt_t1[0:NH, :],
                                           dt_t2[0:NH, :], ALU.add), deps=(t4,))
        t6 = vop(lambda e: e.tensor_scalar(dt_t1[0:NH, :], dts_s[0:NH, :],
                                           nea_s[:, :], None, ALU.mult),
                 deps=(t5, t_nea))
        t_lam = None
        for c in range(NCH):
            t_lam = vop(lambda e, c=c: e.tensor_tensor_scan(
                lam_sb[0:NH, c * L:(c + 1) * L], dt_t1[0:NH, c * L:(c + 1) * L],
                dt_t1[0:NH, c * L:(c + 1) * L], 0.0, ALU.add, ALU.bypass),
                deps=(t6, fr["lamsb"]))
        t_lamsp = dma("sync", sc_lam[half, 0:NH, :], lam_sb[0:NH, :], deps=(t_lam,))
        fr["lamsb"] = t_lamsp
        t_dtT = None
        for c in range(NCH):
            bk, tk = transpose_f32(lambda c=c: dts_s[0:NH, c * L:(c + 1) * L], NH,
                                   deps=(t5, t_eye), in_parts=NH)
            t_dtT = read_bank("vector", bk,
                              lambda e, bk=bk, c=c: e.tensor_copy(
                                  dtT_s[:, c * NH:(c + 1) * NH], pool.f32[bk][:, 0:NH]),
                              deps=(tk, fr["dtT"]))
        fr["dtpass"] = t_dtT

        def conv_chunk(ci, dst_kind, buf):
            t_w = dma("gpsimd", winb[buf][:, :], win_xbc[ci, :, :],
                      deps=(fr["winb"][buf],))
            cideps = (fr["convin"][buf],)
            t_ev = None
            proj_tok = None
            for sub in range(NSUB):
                items = [((lambda kc=kc, buf=buf: winb[buf][:, kc * 128:(kc + 1) * 128]),
                          (lambda kc=kc, sub=sub: xsl(kc, 3 + sub * 512, 3 + (sub + 1) * 512)),
                          0, 512) for kc in range(KC)]
                bk, tk = mm_group(items, deps=(t_w,) + t_x)
                proj_tok = tk
                t_ev = read_bank("scalar", bk,
                                 lambda e, bk=bk, buf=buf, sub=sub: e.activation(
                                     convin[buf][:, 3 + sub * 512: 3 + (sub + 1) * 512],
                                     pool.f32[bk][:, 0:512], AF.Copy),
                                 deps=(tk,) + cideps)
            if half == 0:
                items = [((lambda kc=kc, buf=buf: winb[buf][:, kc * 128:(kc + 1) * 128]),
                          (lambda kc=kc: xsl(kc, 0, 3)), 0, 3) for kc in range(KC)]
                bk2, tk2 = mm_group(items, deps=())
                proj_tok = tk2
                t_h = read_bank("scalar", bk2,
                                lambda e, bk=bk2, buf=buf: e.activation(
                                    convin[buf][:, 0:3], pool.f32[bk][:, 0:3],
                                    AF.Copy),
                                deps=(tk2,) + cideps)
            else:
                t_h = vop(lambda e, ci=ci, buf=buf: e.tensor_copy(
                    convin[buf][:, 0:3], halo_s[:, ci * 4: ci * 4 + 3]),
                    deps=cideps)
            fr["winb"][buf] = proj_tok
            if half == 0:
                vop(lambda e, ci=ci, buf=buf: e.tensor_copy(
                    halo_s[:, ci * 4: ci * 4 + 3], convin[buf][:, TH: TH + 3]),
                    deps=(t_ev,))
            t_dg = None
            for k in range(4):
                t_dg = vop(lambda e, k=k, ci=ci: e.tensor_scalar(
                    diag_s[k][:, :], eye_b[:, :],
                    wcv_s[:, ci * 4 + k: ci * 4 + k + 1], None, ALU.mult),
                    deps=(t_eyeb, t_wcv, fr["conv_pe"]))
            last_cv = None
            conv_tok = None
            for sub in range(NSUB):
                items = [((lambda k=k: diag_s[k][:, :]),
                          (lambda k=k, sub=sub, buf=buf:
                           convin[buf][:, sub * 512 + k: sub * 512 + k + 512]),
                          0, 512) for k in range(4)]
                bk, tk = mm_group(items, deps=(t_ev, t_h, t_dg))
                conv_tok = tk
                last_cv = read_bank("scalar", bk,
                                    lambda e, bk=bk, buf=buf, sub=sub: e.activation(
                                        cstage[buf][:, sub * 512:(sub + 1) * 512],
                                        pool.f32[bk][:, 0:512], AF.Silu),
                                    deps=(tk, fr["cstage"][buf]))
            fr["conv_pe"] = conv_tok
            fr["convin"][buf] = last_cv
            if dst_kind == "B":
                t_cv = vop(lambda e: e.tensor_copy(bfm_s[:, :], cstage[buf][:, :]),
                           deps=(last_cv,))
                fr["cstage"][buf] = t_cv
                return t_cv
            if dst_kind == "C":
                t_cv = vop(lambda e: e.tensor_copy(cfm_s[:, :], cstage[buf][:, :]),
                           deps=(last_cv,))
                fr["cstage"][buf] = t_cv
                return t_cv
            t_sp = dma("sync", sc_xc[dst_kind, half, :, :], cstage[buf][:, :],
                       deps=(last_cv,))
            fr["cstage"][buf] = t_sp
            return t_sp

        t_B = conv_chunk(NH, "B", 0)
        t_C = conv_chunk(NH + 1, "C", 1)
        for h in range(NH):
            conv_chunk(h, h, h % 2)

        zstate = {"zh": -1, "banks": None, "first": None, "toks": None, "tw": None,
                  "buf": 0}

        def z_open(zh):
            zstate["zh"] = zh
            zstate["buf"] = zh % 2
            zstate["tw"] = dma("gpsimd", winb[zstate["buf"]][:, :], win_z[zh, :, :],
                               deps=(fr["winb"][zstate["buf"]],))
            zstate["banks"] = [pool.acquire(pin=True) for _ in range(NSUB)]
            zstate["first"] = [True] * NSUB
            zstate["toks"] = [None] * NSUB

        def z_slice(kcs):
            buf = zstate["buf"]
            for sub in range(NSUB):
                bk = zstate["banks"][sub]
                for kc in kcs:
                    st = zstate["first"][sub]
                    zstate["first"][sub] = False
                    gd = (pool.guard_tok(bk),) if st else ()
                    sp = (kc == KC - 1)
                    zstate["toks"][sub] = b.op(
                        "tensor",
                        lambda e, kc=kc, sub=sub, st=st, sp=sp, bk=bk, buf=buf:
                        e.matmul(pool.f32[bk][:, 0:512],
                                 winb[buf][:, kc * 128:(kc + 1) * 128],
                                 xsl(kc, 3 + sub * 512, 3 + (sub + 1) * 512),
                                 start=st, stop=sp),
                        deps=(zstate["tw"],) + t_x + gd, sem=pool.wr[bk])

        def z_close():
            zh, buf = zstate["zh"], zstate["buf"]
            last = None
            for sub in range(NSUB):
                bk = zstate["banks"][sub]
                last = read_bank("scalar", bk,
                                 lambda e, bk=bk, buf=buf, sub=sub: e.activation(
                                     cstage[buf][:, sub * 512:(sub + 1) * 512],
                                     pool.f32[bk][:, 0:512], AF.Copy),
                                 deps=(zstate["toks"][sub], fr["cstage"][buf]))
                pool.unpin(bk)
            fr["winb"][buf] = zstate["toks"][NSUB - 1]
            t_sp = dma("sync", sc_zs[zh, half, :, :], cstage[buf][:, :], deps=(last,))
            fr["cstage"][buf] = t_sp
            return zstate["toks"][NSUB - 1]

        for c in range(NCH):
            bk, tk = mm1(lambda bk: pool.f32[bk][:, 0:128],
                         lambda c=c: bfm_s[:, c * L:(c + 1) * L],
                         lambda c=c: cfm_s[:, c * L:(c + 1) * L],
                         deps=(t_B, t_C))
            read_bank("vector", bk,
                      lambda e, bk=bk, c=c: e.tensor_tensor(
                          gm_s[:, c * L:(c + 1) * L], pool.f32[bk][:, 0:128],
                          tri_b[:, :], ALU.mult),
                      deps=(tk, t_trib))
            sl = c % 2
            bk7, tok7 = transpose_bf(lambda c=c: bfm_s[:, c * L:(c + 1) * L], 128,
                                     sl, deps=(t_B, t_eyeb))
            read_bank("vector", bk7,
                      lambda e, c=c, sl=sl: e.tensor_copy(
                          bt_s[:, c * L:(c + 1) * L], pool.bf[sl][:, 0:128]),
                      deps=(tok7,))
        t_pre = b.tok(EV)

        ZK = (KC + NCH - 1) // NCH

        hload = {}

        def ssm_load(h):
            buf = h % 2
            ldeps = (fr["lam"][buf], t_lamset, t_lamsp)
            tLl = dma("sync", lamL[buf][1:2, :], sc_lam[half, h:h + 1, :], deps=ldeps)
            tRr = dma("sync", lamR[buf][0:1, :], sc_lam[half, h:h + 1, :], deps=ldeps)
            tX = dma("sync", xcvh[buf][:, :], sc_xc[h, half, :, :],
                     deps=(fr["xcvh"][buf],))
            hload[h] = (tLl, tRr, tX)

        def emit_ssm(h):
            buf = h % 2
            tLl, tRr, tX = hload.pop(h)
            pf = fr.get("ssmstage") or ()
            tE = sop(lambda e, buf=buf: e.activation(erow[buf][:, :],
                                                     lamR[buf][0:1, :], AF.Exp),
                     deps=(tRr, fr["erow"][buf]))

            tr_toks = [None, None]
            for c in range(NCH):
                sl = c // (NCH // 2)
                bk7 = 6 + sl
                gd = (pool.guard_tok(bk7),) if c % (NCH // 2) == 0 else ()
                tr_toks[sl] = b.op(
                    "tensor",
                    lambda e, c=c, buf=buf, sl=sl: e.transpose(
                        pool.bf[sl][:, (c % (NCH // 2)) * 128:
                                     (c % (NCH // 2) + 1) * 128],
                        xcvh[buf][:, c * L:(c + 1) * L], eye_b[:, :]),
                    deps=(tX, t_eyeb) + gd, sem=pool.wr[bk7])

            eb_banks = []
            for q in range(NCH // 4):
                bkE = pool.acquire()
                tkE = b.op("tensor",
                           lambda e, q=q, buf=buf, bk=bkE: e.matmul(
                               pool.f32[bk][:, 0:512], ones_b[:, :],
                               erow[buf][0:1, q * 512:(q + 1) * 512],
                               start=True, stop=True),
                           deps=(tE, pool.guard_tok(bkE)), sem=pool.wr[bkE])
                eb_banks.append((bkE, tkE))
            dst_banks = []
            for q in range(NCH // 4):
                bkT = pool.acquire()
                tkT = None
                gd = (pool.guard_tok(bkT),)
                for j in range(4):
                    c = q * 4 + j
                    tkT = b.op("tensor",
                               lambda e, c=c, j=j, buf=buf, bk=bkT: e.matmul(
                                   pool.f32[bk][:, j * 128:(j + 1) * 128],
                                   lamL[buf][:, c * L:(c + 1) * L],
                                   lamR[buf][:, c * L:(c + 1) * L],
                                   start=True, stop=True),
                               deps=(tLl, tRr) + (gd if j == 0 else ()),
                               sem=pool.wr[bkT])
                dst_banks.append((bkT, tkT))

            tdx = None
            for c in range(NCH):
                sl = c // (NCH // 2)
                tdx = read_bank("scalar", 6 + sl,
                                lambda e, c=c, h=h, sl=sl: e.activation(
                                    dtxB_s[:, c * L:(c + 1) * L],
                                    pool.bf[sl][:, (c % (NCH // 2)) * 128:
                                                 (c % (NCH // 2) + 1) * 128],
                                    AF.Identity,
                                    scale=dtT_s[:, c * NH + h: c * NH + h + 1]),
                                deps=(tr_toks[sl], t_dtT) + pf)
            tM = None
            for q in range(NCH // 4):
                bkT, tkT = dst_banks[q]
                tRl = read_bank("scalar", bkT,
                                lambda e, bk=bkT, q=q: e.activation(
                                    mD_s[:, q * 512:(q + 1) * 512],
                                    pool.f32[bk][:, 0:512], AF.Relu, scale=-1.0),
                                deps=(tkT,) + pf)
                tM = sop(lambda e, q=q: e.activation(
                    mD_s[:, q * 512:(q + 1) * 512], mD_s[:, q * 512:(q + 1) * 512],
                    AF.Exp, scale=-1.0), deps=(tRl,))

            tWm = None
            for q in range(NCH // 4):
                tWm = vop(lambda e, q=q: e.tensor_tensor(
                    wmB_s[:, q * 512:(q + 1) * 512], mD_s[:, q * 512:(q + 1) * 512],
                    gm_s[:, q * 512:(q + 1) * 512], ALU.mult),
                    deps=(tM, t_pre) + pf)
            tCd = None
            tDc = None
            for q in range(NCH // 4):
                bkE, tkE = eb_banks[q]
                tCd = read_bank("vector", bkE,
                                lambda e, bk=bkE, q=q: e.tensor_tensor(
                                    cdB_s[:, q * 512:(q + 1) * 512],
                                    cfm_s[:, q * 512:(q + 1) * 512],
                                    pool.f32[bk][:, 0:512], ALU.mult),
                                deps=(tkE, t_C) + pf)
                tDc = read_bank("vector", bkE,
                                lambda e, bk=bkE, q=q, buf=buf: e.tensor_copy(
                                    dcols[buf][:, q * 4:(q + 1) * 4],
                                    pool.f32[bk][:, 127:512:128]),
                                deps=(tkE, fr["dcp"][buf]) + pf)
            tBF = None
            for c in range(NCH):
                tBF = vop(lambda e, c=c: e.tensor_scalar(
                    bfB_s[:, c * L:(c + 1) * L], bt_s[:, c * L:(c + 1) * L],
                    mD_s[:, c * L + 127: c * L + 128], None, ALU.mult),
                    deps=(tM, t_pre) + pf)

            last_cd2 = last_yp = None
            for c in range(NCH):
                p = c % 2
                bkS, tkS = mm1(lambda bk: pool.f32[bk][:, 0:128],
                               lambda c=c: bfB_s[:, c * L:(c + 1) * L],
                               lambda c=c: dtxB_s[:, c * L:(c + 1) * L],
                               deps=(tBF, tdx))
                last_cd2 = vop(lambda e, c=c, h=h, buf=buf: e.tensor_scalar(
                    cd2strip[buf][:, c * L:(c + 1) * L],
                    cdB_s[:, c * L:(c + 1) * L],
                    dprefst[:, h:h + 1], None, ALU.mult),
                    deps=(tCd, fr["cdsp"]))
                vop(lambda e, c=c, h=h, buf=buf: e.tensor_tensor(
                    dprefst[:, h:h + 1], dprefst[:, h:h + 1],
                    dcols[buf][:, c:c + 1], ALU.mult), deps=(last_cd2, tDc))
                thb = vop(lambda e, h=h, p=p: e.tensor_copy(
                    hb_s[p][:, :], hstate[:, h * 128:(h + 1) * 128]),
                    deps=(hu_tok[h], fr.get(("hb", p))))
                hb_tok[h] = thb
                tHU = read_bank("vector", bkS,
                                lambda e, c=c, h=h, buf=buf, bk=bkS:
                                e.scalar_tensor_tensor(
                                    hstate[:, h * 128:(h + 1) * 128],
                                    hstate[:, h * 128:(h + 1) * 128],
                                    dcols[buf][:, c:c + 1],
                                    pool.f32[bk][:, 0:128], ALU.mult, ALU.add),
                                deps=(tkS, tDc, thb))
                hu_tok[h] = tHU
                z_fill(c)
                items = [((lambda c=c: dtxB_s[:, c * L:(c + 1) * L]),
                          (lambda c=c: wmB_s[:, c * L:(c + 1) * L]), 0, 128),
                         ((lambda p=p: hb_s[p][:, :]),
                          (lambda c=c: cdB_s[:, c * L:(c + 1) * L]), 0, 128)]
                bkY, tkY = mm_group(items, deps=(tdx, tWm, thb, tCd))
                fr[("hb", p)] = tkY
                last_yp = read_bank("vector", bkY,
                                    lambda e, c=c, buf=buf, bk=bkY:
                                    e.scalar_tensor_tensor(
                                        ypstrip[buf][:, c * L:(c + 1) * L],
                                        xcvh[buf][:, c * L:(c + 1) * L],
                                        dall128[:, h:h + 1],
                                        pool.f32[bk][:, 0:128], ALU.mult, ALU.add),
                                    deps=(tkY, t_dall, tX, fr["ypsp"]))
            tsp1 = dma("sync", sc_yp[h, half, :, :], ypstrip[buf][:, :],
                       deps=(last_yp,))
            tsp2 = dma("sync", sc_cd2[h, half, :, :], cd2strip[buf][:, :],
                       deps=(last_cd2,))
            last_yp_sp[h] = tsp1
            last_cd_sp[h] = tsp2
            fr["ypsp"] = tsp1
            fr["cdsp"] = tsp2
            fr["xcvh"][buf] = last_yp
            if half == 1:
                cc1_sp.append(dma("sync", cc1_in[h * 128:(h + 1) * 128, :],
                                  hstate[:, h * 128:(h + 1) * 128],
                                  deps=(hu_tok[h],)))
            fr["lam"][buf] = b.tok(EV)
            fr["erow"][buf] = b.tok(EV)
            fr["dcp"][buf] = b.tok(EV)
            fr["ssmstage"] = (b.tok(EV),)

        zpos = [0]

        def z_fill(c):
            lo = zpos[0]
            hi = min(KC, lo + ZK)
            if lo < hi:
                z_slice(range(lo, hi))
                zpos[0] = hi

        ssm_load(0)
        for h in range(NH):
            z_open(h)
            zpos[0] = 0
            if h + 1 < NH:
                ssm_load(h + 1)
            emit_ssm(h)
            last_z = z_close()
        fr["xb"] = last_z
        fr["dtpass"] = b.tok(EV)

    phase1_half(0)
    phase1_half(1)

    cc_w = list(cc1_sp)
    vop(lambda e: e.memset(dtot_row[:, :], 0.0))
    tdr = vop(lambda e: e.tensor_copy(dtot_row[0:1, 0:NH], dprefst[0:1, 0:NH]),
              deps=(b.tok(EV),))
    cc_w.append(dma("sync", cc1_in[NH * 128:NH * 128 + 1, :], dtot_row[:, :],
                    deps=(tdr,)))
    t_cc1 = b.op("gpsimd",
                 lambda e: e.collective_compute(
                     "AllGather", ALU.bypass,
                     replica_groups=[list(range(NCORES))],
                     ins=[cc1_in.ap()], outs=[cc1_out.ap()]),
                 deps=tuple(cc_w), sem=CC)

    barrier_all()
    cur[0] = PH
    ygn = salloc("ygn", [128, NH * TH], BF16)
    WOW = min(2048, NH * 128)
    NKG = (NH * 128) // WOW
    woutb = [salloc(f"woutb{i}", [128, WOW], BF16) for i in range(2)]
    ypt = [salloc(f"ypt{i}", [128, TH], BF16) for i in range(2)]
    zst = [salloc(f"zst{i}", [128, TH], BF16) for i in range(2)]
    cdt = [salloc(f"cdt{i}", [128, TH], BF16) for i in range(2)]
    sjt = [salloc(f"sjt{i}", [128, 128], F32) for i in range(2)]
    hinf = salloc("hinf", [128, 128], F32)
    hint = [salloc(f"hint{i}", [128, 128], BF16) for i in range(2)]
    dscol_s = salloc("dscol_s", [128, 8], F32)
    dallt = salloc("dallt", [NH, 8], F32)
    ek_s = salloc("ek_s", [NH, 8], F32)
    dsm_s = salloc("dsm_s", [NH, 8], F32)
    dsrow = salloc("dsrow", [1, NH * 8], F32)
    xres = [salloc(f"xres{i}", [128, TH], F32) for i in range(2)]
    x1st = [salloc(f"x1st{i}", [128, 512], F32) for i in range(2)]
    x1bt = [salloc(f"x1bt{i}", [128, 512], BF16) for i in range(2)]
    sqt = [salloc(f"sqt{i}", [128, 512], BF16) for i in range(2)]
    ygt = salloc("ygt", [128, 512], BF16)
    e1t = salloc("e1t", [128, 512], F32)
    rmsB = [salloc(f"rmsB{i}", [128, 512], F32) for i in range(NSUB)]
    assert cur[0] <= SB_BYTES

    slot_free = [None, None]
    dal_toks = []
    for j in range(NCORES):
        dal_toks.append(dma("sync", dallt[:, j:j + 1],
                            cc1_out[j * (NH * 128 + 1) + NH * 128:
                                    j * (NH * 128 + 1) + NH * 128 + 1, 0:NH],
                            deps=(t_cc1,)))
    tE1 = None
    for j in range(NCORES):
        tE1 = vop(lambda e, j=j: e.scalar_tensor_tensor(
            ek_s[0:NH, j:j + 1], dallt[0:NH, j:j + 1], -1.0, cmask_s[0:NH, j:j + 1],
            ALU.add, ALU.mult), deps=(dal_toks[j], t_cm))
        tE1 = vop(lambda e, j=j: e.tensor_scalar(
            ek_s[0:NH, j:j + 1], ek_s[0:NH, j:j + 1], 1.0, None, ALU.add),
            deps=(tE1,))
    tds = vop(lambda e: e.memset(dsm_s[0:NH, :], 1.0), deps=(tE1,))
    for j in range(NCORES - 2, -1, -1):
        tds = vop(lambda e, j=j: e.tensor_tensor(
            dsm_s[0:NH, j:j + 1], dsm_s[0:NH, j + 1:j + 2], ek_s[0:NH, j + 1:j + 2],
            ALU.mult), deps=(tds,))
    for j in range(NCORES):
        tds = vop(lambda e, j=j: e.tensor_tensor(
            dsm_s[0:NH, j:j + 1], dsm_s[0:NH, j:j + 1], cmask_s[0:NH, j:j + 1],
            ALU.mult), deps=(tds,))
    t_dsw = dma("sync", sc_ds[0:NH, :], dsm_s[0:NH, :], deps=(tds,))
    t_dsr = dma("sync", dsrow[0:1, :],
                sc_ds.ap().rearrange("h j -> (h j)")[None, :], deps=(t_dsw,))
    hin_state = {"last": None, "slot": slot_free, "toks": [None] * NH,
                 "stage": [None, None], "dsc": None}

    def emit_hin(h):
        hb2 = h % 2
        bkB, tkB = mm1(lambda bk: pool.f32[bk][:, 0:8],
                       lambda: ones_f[:, :],
                       lambda h=h: dsrow[0:1, h * 8: h * 8 + 8],
                       deps=(t_dsr,))
        tds2 = read_bank("vector", bkB,
                         lambda e, bk=bkB: e.tensor_copy(dscol_s[:, :],
                                                         pool.f32[bk][:, 0:8]),
                         deps=(tkB, hin_state["dsc"]))
        tz = vop(lambda e: e.memset(hinf[:, :], 0.0),
                 deps=(hin_state["last"],) if hin_state["last"] else ())
        for j in range(NCORES):
            tS = dma("sync", sjt[j % 2][:, :],
                     cc1_out[j * (NH * 128 + 1) + h * 128:
                             j * (NH * 128 + 1) + (h + 1) * 128, :],
                     deps=(t_cc1, hin_state["slot"][j % 2]))
            tz = vop(lambda e, j=j: e.scalar_tensor_tensor(
                hinf[:, :], sjt[j % 2][:, :], dscol_s[:, j:j + 1],
                hinf[:, :], ALU.mult, ALU.add),
                deps=(tds2, tS, tz))
            hin_state["slot"][j % 2] = tz
        hin_state["dsc"] = tz
        tok = sop(lambda e, hb2=hb2: e.activation(
            hint[hb2][:, :], hinf[:, :], AF.Copy),
            deps=(tz, hin_state["stage"][hb2]))
        tok = dma("sync", sc_hin[h, :, :], hint[hb2][:, :], deps=(tok,))
        hin_state["stage"][hb2] = tok
        hin_state["last"] = tok
        hin_state["toks"][h] = tok
        return tok

    fr2 = {"yp": [None, None], "zs": [None, None], "cd": [None, None],
           "wout": [None, None], "x1": [None, None], "ygn": None,
           "xres": [None, None], "hin": [None, None]}

    for half in range(2):
        rms_g = [AccGroup(M=1, N=512) for _ in range(NSUB)]
        strips_tok = None
        hl2 = {}

        def heads_load(h):
            buf = h % 2
            if half == 0:
                t_hin_h = emit_hin(h)
            else:
                t_hin_h = hin_state["toks"][h]
            tH = dma("sync", hint[buf][:, :], sc_hin[h, :, :],
                     deps=(t_hin_h, fr2["hin"][buf]))
            tY = dma("sync", ypt[buf][:, :], sc_yp[h, half, :, :],
                     deps=(fr2["yp"][buf], last_yp_sp[h]))
            tZ = dma("sync", zst[buf][:, :], sc_zs[h, half, :, :],
                     deps=(fr2["zs"][buf],))
            tC = dma("sync", cdt[buf][:, :], sc_cd2[h, half, :, :],
                     deps=(fr2["cd"][buf], last_cd_sp[h]))
            hl2[h] = (tH, tY, tZ, tC)

        heads_load(0)
        for h in range(NH):
            buf = h % 2
            if h + 1 < NH:
                heads_load(h + 1)
            tH, tY, tZ, tC = hl2.pop(h)
            tZ = sop(lambda e, buf=buf: e.activation(
                zst[buf][:, :], zst[buf][:, :], AF.Silu), deps=(tZ,))
            tkY = None
            for sub in range(NSUB):
                bkY, tkY = mm1(lambda bk: pool.f32[bk][:, 0:512],
                               lambda buf=buf: hint[buf][:, :],
                               lambda buf=buf, sub=sub:
                               cdt[buf][:, sub * 512:(sub + 1) * 512],
                               deps=(tH, tC))
                te1 = read_bank("vector", bkY,
                                lambda e, bk=bkY, buf=buf, sub=sub: e.tensor_tensor(
                                    e1t[:, :], pool.f32[bk][:, 0:512],
                                    ypt[buf][:, sub * 512:(sub + 1) * 512], ALU.add),
                                deps=(tkY, tY))
                tyg = vop(lambda e, buf=buf, sub=sub: e.tensor_tensor(
                    ygt[:, :], e1t[:, :], zst[buf][:, sub * 512:(sub + 1) * 512],
                    ALU.mult), deps=(te1, tZ))
                tsq = vop(lambda e, sub=sub: e.tensor_tensor(
                    sqt[sub % 2][:, :], ygt[:, :], ygt[:, :], ALU.mult), deps=(tyg,))
                strips_tok = vop(lambda e, h=h, sub=sub: e.tensor_scalar(
                    ygn[:, h * TH + sub * 512: h * TH + (sub + 1) * 512], ygt[:, :],
                    nw_s[:, h:h + 1], None, ALU.mult),
                    deps=(tyg, t_nw, fr2["ygn"]))
                rms_g[sub].add(lambda sub=sub: sqt[sub % 2][:, :],
                               lambda: onescol_b[:, :], deps=(tsq,),
                               last=(h == NH - 1))
            fr2["yp"][buf] = te1
            fr2["zs"][buf] = tsq
            fr2["cd"][buf] = tkY
            fr2["hin"][buf] = tkY
        rms_toks = []
        for sub in range(NSUB):
            g = rms_g[sub]
            lo = half * TH + sub * 512
            tr = read_bank("scalar", g.bk,
                           lambda e, bk=g.bk, lo=lo: e.activation(
                               rmsr[0:1, lo:lo + 512], pool.f32[bk][0:1, 0:512],
                               AF.Ln, bias=cst_s[0:1, 1:2], scale=1.0 / cfg.DSSM),
                           deps=(g.tok,))
            tr2 = sop(lambda e, lo=lo: e.activation(
                rmsr[0:1, lo:lo + 512], rmsr[0:1, lo:lo + 512], AF.Exp, scale=-0.5),
                deps=(tr,))
            g.close()
            bkB, tkB = mm1(lambda bk: pool.f32[bk][:, 0:512],
                           lambda: ones_f[:, :],
                           lambda lo=lo: rmsr[0:1, lo:lo + 512],
                           deps=(tr2,))
            rms_toks.append(read_bank("vector", bkB,
                                      lambda e, bk=bkB, sub=sub: e.tensor_copy(
                                          rmsB[sub][:, :], pool.f32[bk][:, 0:512]),
                                      deps=(tkB,)))
        ln_g = [AccGroup(M=1, N=512) for _ in range(2 * NSUB)]
        stats_fr = {}
        GK = WOW // 128
        for m in range(KC):
            bufm = m % 2
            tXR = dma("sync", xres[bufm][:, :],
                      xt[m * 128:(m + 1) * 128, 3 + half * TH: 3 + half * TH + TH],
                      deps=(fr2["xres"][bufm],))
            acc = [AccGroup(M=128, N=512) for _ in range(NSUB)]
            for g in range(NKG):
                tw = dma("gpsimd", woutb[g % 2][:, :],
                         woutd[m, :, g * WOW:(g + 1) * WOW],
                         deps=(fr2["wout"][g % 2],))
                for sub in range(NSUB):
                    for kk in range(GK):
                        kc = g * GK + kk
                        tk = acc[sub].add(
                            (lambda kc=kc, sub=sub: ygn[:, kc * TH + sub * 512:
                                                        kc * TH + (sub + 1) * 512]),
                            (lambda g=g, kk=kk: woutb[g % 2][:, kk * 128:
                                                             (kk + 1) * 128]),
                            deps=(tw, strips_tok),
                            last=(g == NKG - 1 and kk == GK - 1))
                fr2["wout"][g % 2] = tk
            last_pe = tk
            for sub in range(NSUB):
                bk = acc[sub].bk
                tk = acc[sub].tok
                te = read_bank("vector", bk,
                               lambda e, bk=bk, sub=sub: e.tensor_tensor(
                                   e1t[:, :], pool.f32[bk][:, 0:512],
                                   rmsB[sub][:, :], ALU.mult),
                               deps=(tk, rms_toks[sub]))
                tx1 = vop(lambda e, bufm=bufm, sub=sub: e.tensor_tensor(
                    x1st[bufm][:, :], e1t[:, :],
                    xres[bufm][:, sub * 512:(sub + 1) * 512], ALU.add),
                    deps=(te, tXR, fr2["x1"][bufm]))
                tspx = dma("sync", sc_x1[m, half, :, sub * 512:(sub + 1) * 512],
                           x1st[bufm][:, :], deps=(tx1,))
                fr2["x1"][bufm] = tspx
                tb = vop(lambda e, bufm=bufm: e.tensor_copy(x1bt[bufm][:, :],
                                                            x1st[bufm][:, :]),
                         deps=(tx1, stats_fr.get(("x1b", bufm))))
                tq = vop(lambda e, bufm=bufm, sub=sub: e.tensor_tensor(
                    sqt[sub % 2][:, :], x1bt[bufm][:, :], x1bt[bufm][:, :],
                    ALU.mult), deps=(tb, stats_fr.get(("sq", sub % 2))))
                stats_fr[("x1b", bufm)] = ln_g[sub * 2].add(
                    lambda bufm=bufm: x1bt[bufm][:, :],
                    lambda: onescol_b[:, :], deps=(tb,), last=(m == KC - 1))
                stats_fr[("sq", sub % 2)] = ln_g[sub * 2 + 1].add(
                    lambda sub=sub: sqt[sub % 2][:, :],
                    lambda: onescol_b[:, :], deps=(tq,), last=(m == KC - 1))
                acc[sub].close()
            fr2["xres"][bufm] = tx1
        fr2["ygn"] = last_pe
        for sub in range(NSUB):
            lo = half * TH + sub * 512
            gs, gq = ln_g[sub * 2], ln_g[sub * 2 + 1]
            tmean = read_bank("vector", gs.bk,
                              lambda e, bk=gs.bk, lo=lo: e.tensor_scalar(
                                  mean1r[0:1, lo:lo + 512], pool.f32[bk][0:1, 0:512],
                                  1.0 / cfg.D, None, ALU.mult),
                              deps=(gs.tok,))
            tm2 = vop(lambda e, lo=lo: e.tensor_tensor(
                rstd1r[0:1, lo:lo + 512], mean1r[0:1, lo:lo + 512],
                mean1r[0:1, lo:lo + 512], ALU.mult), deps=(tmean,))
            tvar = read_bank("vector", gq.bk,
                             lambda e, bk=gq.bk, lo=lo: e.scalar_tensor_tensor(
                                 rstd1r[0:1, lo:lo + 512], pool.f32[bk][0:1, 0:512],
                                 1.0 / cfg.D, rstd1r[0:1, lo:lo + 512],
                                 ALU.mult, ALU.subtract),
                             deps=(gq.tok, tm2))
            tl = sop(lambda e, lo=lo: e.activation(
                rstd1r[0:1, lo:lo + 512], rstd1r[0:1, lo:lo + 512], AF.Ln,
                bias=cst_s[0:1, 1:2]), deps=(tvar,))
            sop(lambda e, lo=lo: e.activation(
                rstd1r[0:1, lo:lo + 512], rstd1r[0:1, lo:lo + 512], AF.Exp,
                scale=-0.5), deps=(tl,))
            gs.close()
            gq.close()
    t_ln1 = b.tok(AC)

    barrier_all()
    cur[0] = PH
    wbm_s = salloc("wbm_s", [128, cfg.D], BF16)
    lim_s = salloc("lim_s", [NE, RANK], BF16)
    limt_s = salloc("limt_s", [RANK, NE], BF16)
    w8t_s = salloc("w8t_s", [128, KC * NE], BF16)
    wa_s = salloc("wa_s", [128, KC * RANK], BF16)
    t_wbm = dma("gpsimd", wbm_s[:, :], wbmd[:, :])
    t_lim = dma("gpsimd", lim_s[:, :], limd[:, :])
    t_limt = dma("gpsimd", limt_s[:, :], limtd[:, :])
    t_w8t = dma("gpsimd", w8t_s[:, :], w8td[:, :])
    t_wa = dma("gpsimd", wa_s[:, :], wad[:, :])
    x1bres = salloc("x1bres", [128, KC * TH], BF16)
    wffb = [salloc(f"wffb{i}", [128, KC * 128], BF16) for i in range(2)]
    x1pt = [salloc(f"x1pt{i}", [128, TH], F32) for i in range(2)]
    zr_f = salloc("zr_f", [RANK, T], F32)
    zrb = salloc("zrb", [RANK, T], BF16)
    zp_s = salloc("zp_s", [128, T], BF16)
    wfull = salloc("wfull", [NE, T], BF16)
    hsa = salloc("hsa", [NE, T // 2], F32)
    dsa = salloc("dsa", [NE, T // 2], F32)
    mloc = salloc("mloc", [8, 128], F32)
    mg_s = salloc("mg_s", [1, 16], F32)
    scal_s = salloc("scal_s", [NE, 2], F32)
    lgT = salloc("lgT", [128, NE], F32)
    prb = salloc("prb", [128, NE], F32)
    msk = salloc("msk", [128, NE], F32)
    rcol = salloc("rcol", [128, 4], F32)
    wT_s = salloc("wT_s", [128, NE], F32)
    wfm_s = salloc("wfm_s", [NE, 128], F32)
    m1B = [salloc(f"m1B{i}", [128, 512], F32) for i in range(NSUB)]
    r1B = [salloc(f"r1B{i}", [128, 512], F32) for i in range(NSUB)]
    x1ft = salloc("x1ft", [128, 512], F32)
    e2t = salloc("e2t", [128, 512], F32)
    rstage = [salloc(f"rstage{i}", [128, 512], F32) for i in range(2)]
    rbt = [salloc(f"rbt{i}", [128, 512], BF16) for i in range(2)]
    sq2t = [salloc(f"sq2t{i}", [128, 512], BF16) for i in range(2)]
    ostage = [salloc(f"ostage{i}", [128, TH], F32) for i in range(2)]
    ln2m = salloc("ln2m", [1, T], F32)
    ln2r = salloc("ln2r", [1, T], F32)
    xbt2 = [salloc(f"xbt2_{i}", [128, 512], BF16) for i in range(2)]
    assert cur[0] <= SB_BYTES, cur[0]

    t_ml = vop(lambda e: e.memset(mloc[:, :], -1e30))
    t_zp0 = vop(lambda e: e.memset(zp_s[:, :], 0.0))

    def build_ln1_bcast(half, deps):
        deps = tuple(deps)
        mt, rt = [], []
        for sub in range(NSUB):
            lo = half * TH + sub * 512
            bkM, tkM = mm1(lambda bk: pool.f32[bk][:, 0:512],
                           lambda: ones_f[:, :],
                           lambda lo=lo: mean1r[0:1, lo:lo + 512], deps=deps)
            mt.append(read_bank("vector", bkM,
                                lambda e, bk=bkM, sub=sub: e.tensor_copy(
                                    m1B[sub][:, :], pool.f32[bk][:, 0:512]),
                                deps=(tkM,)))
            bkR, tkR = mm1(lambda bk: pool.f32[bk][:, 0:512],
                           lambda: ones_f[:, :],
                           lambda lo=lo: rstd1r[0:1, lo:lo + 512], deps=deps)
            rt.append(read_bank("vector", bkR,
                                lambda e, bk=bkR, sub=sub: e.tensor_copy(
                                    r1B[sub][:, :], pool.f32[bk][:, 0:512]),
                                deps=(tkR,)))
        return mt, rt

    fr3 = {"x1p": [None, None], "x1b": [None, None], "mb": ()}
    t_max = t_ml
    zh_done = []
    for half in range(2):
        mt, rt = build_ln1_bcast(half, (t_ln1,) + fr3["mb"])
        zg = [AccGroup(M=RANK, N=512) for _ in range(NSUB)]
        hg = [AccGroup(M=NE, N=512) for _ in range(NSUB)]
        prev_pe = None
        xl = {}

        def x1_load(m):
            buf = m % 2
            xl[m] = dma("sync", x1pt[buf][:, :], sc_x1[m, half, :, :],
                        deps=(fr3["x1p"][buf],))

        x1_load(0)
        for m in range(KC):
            buf = m % 2
            if m + 1 < KC:
                x1_load(m + 1)
            tx = xl.pop(m)
            for sub in range(NSUB):
                tb = vop(lambda e, buf=buf, sub=sub: e.scalar_tensor_tensor(
                    xbt2[0][:, :], x1pt[buf][:, sub * 512:(sub + 1) * 512], 1.0,
                    m1B[sub][:, :], ALU.mult, ALU.subtract),
                    deps=(tx, mt[sub], fr3["x1b"][0], prev_pe))
                tb = vop(lambda e, sub=sub: e.tensor_tensor(
                    xbt2[0][:, :], xbt2[0][:, :], r1B[sub][:, :], ALU.mult),
                    deps=(tb, rt[sub]))
                tsp = dma("sync", sc_x1b[m, half, :, sub * 512:(sub + 1) * 512],
                          xbt2[0][:, :], deps=(tb,))
                fr3["x1b"][0] = tsp
                prev_pe = zg[sub].add(lambda: xbt2[0][:, :],
                                      lambda m=m: wa_s[:, m * RANK:(m + 1) * RANK],
                                      deps=(tb, t_wa), last=(m == KC - 1))
                prev_pe = hg[sub].add(lambda: xbt2[0][:, :],
                                      lambda m=m: w8t_s[:, m * NE:(m + 1) * NE],
                                      deps=(tb, t_w8t), last=(m == KC - 1))
            fr3["x1p"][buf] = tb
        for sub in range(NSUB):
            lo = half * TH + sub * 512
            tz1 = read_bank("vector", zg[sub].bk,
                            lambda e, bk=zg[sub].bk, lo=lo: e.tensor_copy(
                                zr_f[0:RANK, lo:lo + 512],
                                pool.f32[bk][0:RANK, 0:512]),
                            deps=(zg[sub].tok,))
            tz2 = vop(lambda e, lo=lo: e.tensor_copy(
                zrb[0:RANK, lo:lo + 512], zr_f[0:RANK, lo:lo + 512]), deps=(tz1,))
            th1 = read_bank("vector", hg[sub].bk,
                            lambda e, bk=hg[sub].bk, lo=lo: e.tensor_copy(
                                hsa[0:NE, lo // 2: lo // 2 + 256],
                                pool.f32[bk][0:NE, 1:512:2]),
                            deps=(hg[sub].tok,))
            zg[sub].close()
            hg[sub].close()
            bkD, tkD = mm1(lambda bk: pool.f32[bk][0:NE, 0:256],
                           lambda: limt_s[:, :],
                           lambda lo=lo: zrb[0:RANK, lo + 1: lo + 512: 2],
                           deps=(tz2, t_limt))
            td1 = read_bank("vector", bkD,
                            lambda e, bk=bkD, lo=lo: e.tensor_copy(
                                dsa[0:NE, lo // 2: lo // 2 + 256],
                                pool.f32[bk][0:NE, 0:256]),
                            deps=(tkD,))
            tm1 = vop(lambda e, lo=lo: e.reduce_max(
                mloc[0:NE, 2:3], hsa[0:NE, lo // 2: lo // 2 + 256], axis=AXX,
                apply_absolute_value=True), deps=(th1, t_ml))
            t_max = vop(lambda e: e.tensor_tensor(
                mloc[0:NE, 0:1], mloc[0:NE, 0:1], mloc[0:NE, 2:3], ALU.max),
                deps=(tm1,))
            tm2 = vop(lambda e, lo=lo: e.reduce_max(
                mloc[0:NE, 3:4], dsa[0:NE, lo // 2: lo // 2 + 256], axis=AXX,
                apply_absolute_value=True), deps=(td1,))
            t_max = vop(lambda e: e.tensor_tensor(
                mloc[0:NE, 1:2], mloc[0:NE, 1:2], mloc[0:NE, 3:4], ALU.max),
                deps=(tm2,))
        fr3["mb"] = (t_max,)

    bkT2, tkT2 = transpose_f32(lambda: mloc[0:8, :], 8, deps=(t_max, t_eye),
                               in_parts=8)
    tcm = read_bank("vector", bkT2,
                    lambda e, bk=bkT2: e.reduce_max(rcol[:, 0:1],
                                                    pool.f32[bk][:, 0:8], axis=AXX),
                    deps=(tkT2,))
    tz8 = vop(lambda e: e.memset(z8_s[:, :], 0.0))
    tz8 = vop(lambda e: e.tensor_copy(z8_s[0:2, 0:1], rcol[0:2, 0:1]),
              deps=(tcm, tz8))
    tw2 = dma("sync", cc2_in[:, :], z8_s[:, :], deps=(tz8,))
    t_cc2 = b.op("gpsimd",
                 lambda e: e.collective_compute(
                     "AllGather", ALU.bypass,
                     replica_groups=[list(range(NCORES))],
                     ins=[cc2_in.ap()], outs=[cc2_out.ap()]),
                 deps=(tw2,), sem=CC)
    tmg = dma("sync", mg_s[0:1, 0:16],
              cc2_out.ap().rearrange("(j r) c -> j r c", r=8)[:, 0:2, 0:1],
              deps=(t_cc2,), slow=True)
    tgm = vop(lambda e: e.tensor_reduce(
        rcol[0:1, 0:2], mg_s[0:1, 0:16].rearrange("a (j c) -> a c j", c=2),
        AXX, ALU.max), deps=(tmg,))
    tsc = vop(lambda e: e.tensor_scalar(rcol[0:1, 0:2], rcol[0:1, 0:2], 1e-6, None,
                                        ALU.max), deps=(tgm,))
    tsc = vop(lambda e: e.reciprocal(rcol[0:1, 2:4], rcol[0:1, 0:2]), deps=(tsc,))
    tsc = vop(lambda e: e.tensor_scalar(rcol[0:1, 2:4], rcol[0:1, 2:4], 0.5, None,
                                        ALU.mult), deps=(tsc,))
    bkB2, tkB2 = mm1(lambda bk: pool.f32[bk][0:NE, 0:2],
                     lambda: ones_f[0:1, 0:NE],
                     lambda: rcol[0:1, 2:4], deps=(tsc,))
    t_scal = read_bank("vector", bkB2,
                       lambda e, bk=bkB2: e.tensor_copy(scal_s[0:NE, :],
                                                        pool.f32[bk][0:NE, 0:2]),
                       deps=(tkB2,))

    t_wf = None
    for a in range(T // 256):
        tl1 = vop(lambda e, a=a: e.tensor_scalar(
            dsa[0:NE, a * 128:(a + 1) * 128], dsa[0:NE, a * 128:(a + 1) * 128],
            scal_s[0:NE, 1:2], None, ALU.mult), deps=(t_scal,))
        tl2 = vop(lambda e, a=a: e.scalar_tensor_tensor(
            hsa[0:NE, a * 128:(a + 1) * 128], hsa[0:NE, a * 128:(a + 1) * 128],
            scal_s[0:NE, 0:1], dsa[0:NE, a * 128:(a + 1) * 128], ALU.mult, ALU.add),
            deps=(tl1,))
        bkt, tkt = transpose_f32(lambda a=a: hsa[0:NE, a * 128:(a + 1) * 128], NE,
                                 deps=(tl2, t_eye), in_parts=NE)
        tt1 = read_bank("vector", bkt,
                        lambda e, bk=bkt: e.tensor_copy(lgT[:, 0:NE],
                                                        pool.f32[bk][:, 0:NE]),
                        deps=(tkt,))
        ts1 = vop(lambda e: e.reduce_max(rcol[:, 0:1], lgT[:, 0:NE], axis=AXX),
                  deps=(tt1,))
        ts2 = vop(lambda e: e.tensor_scalar(rcol[:, 1:2], rcol[:, 0:1], -1.0, None,
                                            ALU.mult), deps=(ts1,))
        ts3 = sop(lambda e: e.activation(prb[:, 0:NE], lgT[:, 0:NE], AF.Exp,
                                         bias=rcol[:, 1:2]), deps=(ts2,))
        ts4 = vop(lambda e: e.reduce_sum(rcol[:, 0:1], prb[:, 0:NE], axis=AXX),
                  deps=(ts3,))
        ts5 = vop(lambda e: e.reciprocal(rcol[:, 1:2], rcol[:, 0:1]), deps=(ts4,))
        ts6 = vop(lambda e: e.tensor_scalar(prb[:, 0:NE], prb[:, 0:NE],
                                            rcol[:, 1:2], None, ALU.mult),
                  deps=(ts5,))
        tt2 = vop(lambda e: e.reduce_max(rcol[:, 0:1], prb[:, 0:NE], axis=AXX),
                  deps=(ts6,))
        tt3 = vop(lambda e: e.tensor_scalar(msk[:, 0:NE], prb[:, 0:NE],
                                            rcol[:, 0:1], None, ALU.is_ge),
                  deps=(tt2,))
        tt4 = vop(lambda e: e.scalar_tensor_tensor(msk[:, 0:NE], msk[:, 0:NE], -2.0,
                                                   prb[:, 0:NE], ALU.mult, ALU.add),
                  deps=(tt3,))
        tt5 = vop(lambda e: e.reduce_max(rcol[:, 0:1], msk[:, 0:NE], axis=AXX),
                  deps=(tt4,))
        tt6 = vop(lambda e: e.tensor_scalar(rcol[:, 1:2], rcol[:, 0:1], -2.0, None,
                                            ALU.mult), deps=(tt5,))
        tt7 = sop(lambda e: e.activation(msk[:, 0:NE], prb[:, 0:NE], AF.Sigmoid,
                                         scale=2.0, bias=rcol[:, 1:2]), deps=(tt6,))
        tt8 = vop(lambda e: e.tensor_tensor(msk[:, 0:NE], prb[:, 0:NE],
                                            msk[:, 0:NE], ALU.mult), deps=(tt7,))
        tt9 = vop(lambda e: e.reduce_sum(rcol[:, 0:1], msk[:, 0:NE], axis=AXX),
                  deps=(tt8,))
        tta = vop(lambda e: e.tensor_scalar(rcol[:, 0:1], rcol[:, 0:1], 1e-9, None,
                                            ALU.add), deps=(tt9,))
        ttb = vop(lambda e: e.reciprocal(rcol[:, 1:2], rcol[:, 0:1]), deps=(tta,))
        ttc = vop(lambda e: e.tensor_scalar(wT_s[:, 0:NE], msk[:, 0:NE],
                                            rcol[:, 1:2], None, ALU.mult),
                  deps=(ttb,))
        bkw, tkw = transpose_f32(lambda: wT_s[:, 0:NE], 128, deps=(ttc,),
                                 out_parts=NE)
        ttd = read_bank("vector", bkw,
                        lambda e, bk=bkw: e.tensor_copy(wfm_s[0:NE, :],
                                                        pool.f32[bk][0:NE, 0:128]),
                        deps=(tkw,))
        t_wf = vop(lambda e, a=a: e.tensor_copy(
            wfull[0:NE, a * 256: (a + 1) * 256: 2], wfm_s[0:NE, :]), deps=(ttd,))
        t_wf = vop(lambda e, a=a: e.tensor_copy(
            wfull[0:NE, a * 256 + 1: (a + 1) * 256: 2], wfm_s[0:NE, :]),
            deps=(t_wf,))
    t_zp = None
    for q in range(T // 512):
        bkP, tkP = mm1(lambda bk: pool.f32[bk][0:RANK, 0:512],
                       lambda: lim_s[:, :],
                       lambda q=q: wfull[0:NE, q * 512:(q + 1) * 512],
                       deps=(t_wf, t_lim))
        t_zp = read_bank("vector", bkP,
                         lambda e, bk=bkP, q=q: e.scalar_tensor_tensor(
                             zp_s[0:RANK, q * 512:(q + 1) * 512],
                             zr_f[0:RANK, q * 512:(q + 1) * 512], 0.5,
                             pool.f32[bk][0:RANK, 0:512], ALU.mult, ALU.mult),
                         deps=(tkP, t_zp0))

    fr4 = {"wff": [None, None], "x1p": [None, None], "rst": [None, None],
           "x1c": None, "mb": fr3["mb"]}
    for half in range(2):
        mt, rt = build_ln1_bcast(half, (t_ln1,) + tuple(
            t for t in fr4["mb"] if t))
        t_xc_l = []
        for m in range(KC):
            t_xc_l.append(dma("sync", x1bres[:, m * TH:(m + 1) * TH],
                              sc_x1b[m, half, :, :],
                              deps=(fr4["x1c"],) if m == 0 else ()))
        t_xc = tuple(t_xc_l)
        ln_g = [AccGroup(M=1, N=512) for _ in range(2 * NSUB)]
        stats2_fr = {}
        last_tv = [None, None]
        wl = {}

        def ff_load(m):
            buf = m % 2
            t_w = dma("gpsimd", wffb[buf][:, :], wffd[m, :, :],
                      deps=(fr4["wff"][buf],))
            t_xp = dma("sync", x1pt[buf][:, :], sc_x1[m, half, :, :],
                       deps=(fr4["x1p"][buf],))
            wl[m] = (t_w, t_xp)

        ff_load(0)
        for m in range(KC):
            buf = m % 2
            if m + 1 < KC:
                ff_load(m + 1)
            t_w, t_xp = wl.pop(m)
            ptok = None
            for sub in range(NSUB):
                lo = half * TH + sub * 512
                items = [((lambda kc=kc, buf=buf:
                           wffb[buf][:, kc * 128:(kc + 1) * 128]),
                          (lambda kc=kc, sub=sub:
                           x1bres[:, kc * TH + sub * 512: kc * TH + (sub + 1) * 512]),
                          0, 512) for kc in range(KC)]
                items.append(((lambda m=m: wbm_s[:, m * 128:(m + 1) * 128]),
                              (lambda lo=lo: zp_s[:, lo:lo + 512]), 0, 512))
                bk, tk = mm_group(items, deps=(t_w, t_zp, t_wbm) + t_xc)
                ptok = tk
                te = read_bank("scalar", bk,
                               lambda e, bk=bk, m=m: e.activation(
                                   e2t[:, :], pool.f32[bk][:, 0:512], AF.Identity,
                                   bias=bcol_s[:, m:m + 1]), deps=(tk, t_bc))
                tv = vop(lambda e, buf=buf, sub=sub: e.scalar_tensor_tensor(
                    x1ft[:, :], x1pt[buf][:, sub * 512:(sub + 1) * 512], 1.0,
                    m1B[sub][:, :], ALU.mult, ALU.subtract),
                    deps=(t_xp, mt[sub], fr4.get("x1ft")))
                tv = vop(lambda e, sub=sub: e.tensor_tensor(
                    x1ft[:, :], x1ft[:, :], r1B[sub][:, :], ALU.mult),
                    deps=(tv, rt[sub]))
                tR = vop(lambda e, sub=sub: e.tensor_tensor(
                    rstage[sub % 2][:, :], e2t[:, :], x1ft[:, :], ALU.add),
                    deps=(te, tv, fr4["rst"][sub % 2]))
                fr4["x1ft"] = tR
                tsp = dma("sync", sc_r[m, half, :, sub * 512:(sub + 1) * 512],
                          rstage[sub % 2][:, :], deps=(tR,))
                fr4["rst"][sub % 2] = tsp
                trb = vop(lambda e, sub=sub: e.tensor_copy(rbt[sub % 2][:, :],
                                                           rstage[sub % 2][:, :]),
                          deps=(tR, stats2_fr.get(("rb", sub % 2))))
                tq = vop(lambda e, sub=sub: e.tensor_tensor(
                    sq2t[sub % 2][:, :], rbt[sub % 2][:, :], rbt[sub % 2][:, :],
                    ALU.mult), deps=(trb, stats2_fr.get(("sq", sub % 2))))
                stats2_fr[("rb", sub % 2)] = ln_g[sub * 2].add(
                    lambda sub=sub: rbt[sub % 2][:, :],
                    lambda: onescol_b[:, :], deps=(trb,), last=(m == KC - 1))
                stats2_fr[("sq", sub % 2)] = ln_g[sub * 2 + 1].add(
                    lambda sub=sub: sq2t[sub % 2][:, :],
                    lambda: onescol_b[:, :], deps=(tq,), last=(m == KC - 1))
                last_tv[buf] = tv
            fr4["wff"][buf] = ptok
            fr4["x1p"][buf] = last_tv[buf]
        fr4["x1c"] = ptok
        for sub in range(NSUB):
            lo = half * TH + sub * 512
            gs, gq = ln_g[sub * 2], ln_g[sub * 2 + 1]
            tmean = read_bank("vector", gs.bk,
                              lambda e, bk=gs.bk, lo=lo: e.tensor_scalar(
                                  ln2m[0:1, lo:lo + 512], pool.f32[bk][0:1, 0:512],
                                  1.0 / cfg.D, None, ALU.mult),
                              deps=(gs.tok,))
            tm2 = vop(lambda e, lo=lo: e.tensor_tensor(
                ln2r[0:1, lo:lo + 512], ln2m[0:1, lo:lo + 512],
                ln2m[0:1, lo:lo + 512], ALU.mult), deps=(tmean,))
            tvar = read_bank("vector", gq.bk,
                             lambda e, bk=gq.bk, lo=lo: e.scalar_tensor_tensor(
                                 ln2r[0:1, lo:lo + 512], pool.f32[bk][0:1, 0:512],
                                 1.0 / cfg.D, ln2r[0:1, lo:lo + 512],
                                 ALU.mult, ALU.subtract),
                             deps=(gq.tok, tm2))
            tl = sop(lambda e, lo=lo: e.activation(
                ln2r[0:1, lo:lo + 512], ln2r[0:1, lo:lo + 512], AF.Ln,
                bias=cst_s[0:1, 1:2]), deps=(tvar,))
            sop(lambda e, lo=lo: e.activation(
                ln2r[0:1, lo:lo + 512], ln2r[0:1, lo:lo + 512], AF.Exp, scale=-0.5),
                deps=(tl,))
            gs.close()
            gq.close()
        fr4["mb"] = (b.tok(EV), b.tok(AC))
    t_ln2 = b.tok(AC)

    fr5 = {"x1p": [None, None], "ost": [None, None], "mb": fr4["mb"]}
    for half in range(2):
        mt, rt = [], []
        for sub in range(NSUB):
            lo = half * TH + sub * 512
            bkM, tkM = mm1(lambda bk: pool.f32[bk][:, 0:512],
                           lambda: ones_f[:, :],
                           lambda lo=lo: ln2m[0:1, lo:lo + 512],
                           deps=(t_ln2,) + tuple(t for t in fr5["mb"] if t))
            mt.append(read_bank("vector", bkM,
                                lambda e, bk=bkM, sub=sub: e.tensor_copy(
                                    m1B[sub][:, :], pool.f32[bk][:, 0:512]),
                                deps=(tkM,)))
            bkR, tkR = mm1(lambda bk: pool.f32[bk][:, 0:512],
                           lambda: ones_f[:, :],
                           lambda lo=lo: ln2r[0:1, lo:lo + 512],
                           deps=(t_ln2,) + tuple(t for t in fr5["mb"] if t))
            rt.append(read_bank("vector", bkR,
                                lambda e, bk=bkR, sub=sub: e.tensor_copy(
                                    r1B[sub][:, :], pool.f32[bk][:, 0:512]),
                                deps=(tkR,)))
        rl = {}

        def r_load(m):
            buf = m % 2
            rl[m] = dma("sync", x1pt[buf][:, :], sc_r[m, half, :, :],
                        deps=(fr5["x1p"][buf],))

        r_load(0)
        for m in range(KC):
            buf = m % 2
            if m + 1 < KC:
                r_load(m + 1)
            t_r = rl.pop(m)
            to = fr5.get("x1ft")
            for sub in range(NSUB):
                to = vop(lambda e, buf=buf, sub=sub: e.scalar_tensor_tensor(
                    x1ft[:, :], x1pt[buf][:, sub * 512:(sub + 1) * 512], 1.0,
                    m1B[sub][:, :], ALU.mult, ALU.subtract),
                    deps=(t_r, mt[sub], to))
                to = vop(lambda e, buf=buf, sub=sub: e.tensor_tensor(
                    ostage[buf][:, sub * 512:(sub + 1) * 512], x1ft[:, :],
                    r1B[sub][:, :], ALU.mult),
                    deps=(to, rt[sub], fr5["ost"][buf]))
            fr5["x1ft"] = to
            t_o = dma_out(ytd[m * 128:(m + 1) * 128, half * TH: half * TH + TH],
                          ostage[buf][:, :], deps=(to,))
            fr5["ost"][buf] = t_o
            fr5["x1p"][buf] = to
        fr5["mb"] = (b.tok(EV),)

    final_deps = tuple((s, b.cnt[s]) for q in qlanes for s in qlanes[q])
    b.op("sync", lambda e: e.nop(), deps=final_deps)

    with nc.Block() as block:
        b.finalize(block)
    b.es.close()
    return nc


def kernel(**inputs):
    cfg = FULL
    per_core = shard_inputs(cfg, inputs)
    nc = build_program(cfg)
    res = run_bass_kernel_spmd(nc, per_core, core_ids=list(range(NCORES)))
    outs = [np.ascontiguousarray(res.results[i]["yt"].T) for i in range(NCORES)]
    return np.concatenate(outs, axis=0).reshape(1, cfg.S, cfg.D).astype(np.float32)



def _ref_numpy(cfg, inp):
    x = inp["L0_attn_hidden_states"].reshape(cfg.S, cfg.D).astype(np.float64)
    W_in = inp["L0_attn_W_in"].astype(np.float64)
    conv_w = inp["L0_attn_conv_w"].astype(np.float64)
    dtb = inp["L0_attn_dt_bias"].astype(np.float64)
    alog = inp["L0_attn_A_log"].astype(np.float64)
    Dp = inp["L0_attn_D_param"].astype(np.float64)
    nw = inp["L0_attn_norm_weight"].astype(np.float64)
    W_out = inp["L0_attn_W_out"].astype(np.float64)
    Wf = inp["L0_ffn_W"].astype(np.float64)
    bf = inp["L0_ffn_b"].astype(np.float64)
    Af = inp["L0_ffn_A"].astype(np.float64)
    Bm = inp["L0_ffn_Bmat"].astype(np.float64)
    Li = inp["L0_ffn_LiMEs"].astype(np.float64)
    NH, DSSM, CONV = cfg.NH, cfg.DSSM, cfg.CONV

    def silu(v):
        return v / (1.0 + np.exp(-v))

    proj = x @ W_in
    z = proj[:, :DSSM]
    xBC = proj[:, DSSM:DSSM + CONV]
    dt = proj[:, DSSM + CONV:]
    pad = np.vstack([np.zeros((3, CONV)), xBC])
    conv = sum(pad[k:k + cfg.S] * conv_w[:, k] for k in range(4))
    xBC = silu(conv)
    xs = xBC[:, :DSSM].reshape(cfg.S, NH, 128)
    Bmv = xBC[:, DSSM:DSSM + 128]
    Cmv = xBC[:, DSSM + 128:]
    dt = np.log1p(np.exp(-(np.abs(dt + dtb[None, :])))) + np.maximum(dt + dtb[None, :], 0)
    dA = np.exp(dt * -np.exp(alog)[None, :])
    h = np.zeros((NH, 128, 128))
    ys = np.zeros((cfg.S, NH, 128))
    for t in range(cfg.S):
        h = dA[t][:, None, None] * h + (dt[t][:, None] * xs[t])[:, :, None] * Bmv[t][None, None, :]
        ys[t] = h @ Cmv[t]
    y = ys + Dp[None, :, None] * xs
    y = y.reshape(cfg.S, DSSM) * silu(z)
    var = np.mean(y * y, axis=-1, keepdims=True)
    y = y / np.sqrt(var + 1e-5) * nw[None, :]
    attn = y @ W_out

    def ln(v):
        m = v.mean(-1, keepdims=True)
        s = v.var(-1, keepdims=True)
        return (v - m) / np.sqrt(s + 1e-5)

    x1 = ln(x + attn)
    H = x1 @ Wf.T + bf
    zr = x1 @ Af
    anchors = np.arange(1, cfg.S, 2)
    Hs = H[anchors][:, :cfg.NE]
    Ds = zr[anchors] @ Li.T
    hsc = max(np.abs(Hs).max(), 1e-6)
    dsc = max(np.abs(Ds).max(), 1e-6)
    lg = 0.5 * Hs / hsc + 0.5 * Ds / dsc
    p = np.exp(lg - lg.max(-1, keepdims=True))
    p = p / p.sum(-1, keepdims=True)
    thr = np.sort(p, axis=-1)[:, -2:-1]
    mask = 1.0 / (1.0 + np.exp(-(p - thr) / 0.5))
    w = p * mask
    w = w / (w.sum(-1, keepdims=True) + 1e-9)
    wfull = np.repeat(w, 2, axis=0)
    pm = wfull @ Li
    delta = (zr * pm) @ Bm * 0.5
    out = ln(x1 + H + delta)
    return out.reshape(1, cfg.S, cfg.D)


def _gen_mini_inputs(cfg, seed=0):
    rng = np.random.default_rng(seed)
    f = lambda *s: rng.standard_normal(s).astype(np.float32)
    return {
        "L0_attn_hidden_states": f(1, cfg.S, cfg.D),
        "L0_attn_W_in": (f(cfg.D, cfg.PROJ) * 0.02).astype(np.float32),
        "L0_attn_conv_w": (f(cfg.CONV, 4) * 0.2).astype(np.float32),
        "L0_attn_dt_bias": f(cfg.NH),
        "L0_attn_A_log": np.log(rng.uniform(1, 16, cfg.NH)).astype(np.float32),
        "L0_attn_D_param": f(cfg.NH),
        "L0_attn_norm_weight": (1 + 0.1 * f(cfg.DSSM)).astype(np.float32),
        "L0_attn_W_out": (f(cfg.DSSM, cfg.D) * 0.02).astype(np.float32),
        "L0_ffn_W": (f(cfg.D, cfg.D) / np.sqrt(cfg.D)).astype(np.float32),
        "L0_ffn_b": (f(cfg.D) * 0.01).astype(np.float32),
        "L0_ffn_A": (f(cfg.D, cfg.RANK) / np.sqrt(cfg.D)).astype(np.float32),
        "L0_ffn_Bmat": (f(cfg.RANK, cfg.D) * 0.01).astype(np.float32),
        "L0_ffn_LiMEs": (f(cfg.NE, cfg.RANK) * 0.1).astype(np.float32),
    }


def selftest_mini(use_sim=True, cfg=None):
    cfg = cfg or MINI
    inp = _gen_mini_inputs(cfg)
    per_core = shard_inputs(cfg, inp)
    nc = build_program(cfg)
    if use_sim:
        from concourse import bass_interp
        sim = bass_interp.MultiCoreSim(nc, NCORES)
        for i in range(NCORES):
            for k, v in per_core[i].items():
                sim.cores[i].tensor(k)[:] = v
        sim.simulate()
        outs = [np.ascontiguousarray(np.asarray(sim.cores[i].mem_tensor("yt")).T)
                for i in range(NCORES)]
    else:
        res = run_bass_kernel_spmd(nc, per_core, core_ids=list(range(NCORES)))
        outs = [np.ascontiguousarray(res.results[i]["yt"].T) for i in range(NCORES)]
    got = np.concatenate(outs, axis=0).reshape(1, cfg.S, cfg.D)
    ref = _ref_numpy(cfg, inp)
    err = got - ref
    rv = float(np.mean(err ** 2) / np.mean(ref ** 2))
    print("mini residual-variance ratio:", rv)
    print("mini max abs err:", float(np.abs(err).max()),
          " ref scale:", float(np.abs(ref).max()))
    return rv


if __name__ == "__main__":
    import sys
    if "--mini-hw" in sys.argv:
        selftest_mini(use_sim=False)
    elif "--mid" in sys.argv:
        selftest_mini(use_sim=True, cfg=MID)
    else:
        selftest_mini(use_sim=True)
